# version 26
# baseline (speedup 1.0000x reference)
import numpy as np
import concourse.bass as bass
import concourse.mybir as mybir
from concourse.bass_utils import run_bass_kernel_spmd

SIM_SAFE = False

F32 = mybir.dt.float32
BF16 = mybir.dt.bfloat16
ALU = mybir.AluOpType
ACT = mybir.ActivationFunctionType

NCORES = 8
T = 1024
TP = 1152
D = 4096
S = 16
H = 2048
FF = 4096
NDC = 32
NHC = 16
NTC = 8
EPS = 1e-5


class Bld:
    def __init__(self, nc):
        self.nc = nc
        self.cnt = {}
        self.guar = {}

    def sem(self, name):
        s = self.nc.semaphore(name).__enter__()
        self.cnt[name] = 0
        self.all_sems = getattr(self, "all_sems", [])
        self.all_sems.append(s)
        return s

    def inc(self, instr, sem, amt=1):
        instr.then_inc(sem, amt)
        self.cnt[sem.name] += amt
        return instr

    def snap(self, *sems):
        return {s: self.cnt[s.name] for s in sems}

    def wait(self, eng, deps, tag):
        if not deps:
            return
        g = self.guar.setdefault(tag, {})
        for sem, v in deps.items():
            if v > 0 and g.get(sem.name, 0) < v:
                eng.wait_ge(sem, v)
                g[sem.name] = v


class DStream:

    def __init__(self, bs, name, nslots):
        self.bs = bs
        self.sems = [bs.sem(f"{name}{i}") for i in range(nslots)]
        self.n = nslots

    def dma(self, eng, slot, out, in_, **kw):
        s = self.sems[slot % self.n]
        self.bs.inc(eng.dma_start(out=out, in_=in_, **kw), s, 16)

    def snap(self, slot):
        s = self.sems[slot % self.n]
        return {s: self.bs.cnt[s.name]}

    def snap_all(self):
        return {s: self.bs.cnt[s.name] for s in self.sems}


def build():
    nc = bass.Bass("TRN2", target_bir_lowering=False,
                   detect_race_conditions=SIM_SAFE)
    bs = Bld(nc)

    x_pad = nc.dram_tensor("x_pad", [TP, H], F32, kind="ExternalInput").ap()
    w_in_t = nc.dram_tensor("w_in_t", [64, 16, 128, 128], F32, kind="ExternalInput").ap()
    conv_diag = nc.dram_tensor("conv_diag", [NDC, 3, 128, 128], F32, kind="ExternalInput").ap()
    conv_b_t = nc.dram_tensor("conv_b_t", [128, NDC], F32, kind="ExternalInput").ap()
    w_x = nc.dram_tensor("w_x", [D, 160], F32, kind="ExternalInput").ap()
    w_dt_t = nc.dram_tensor("w_dt_t", [NDC, 128, 128], F32, kind="ExternalInput").ap()
    b_dt_t = nc.dram_tensor("b_dt_t", [128, NDC], F32, kind="ExternalInput").ap()
    a_log = nc.dram_tensor("a_log", [D, S], F32, kind="ExternalInput").ap()
    d_t = nc.dram_tensor("d_t", [128, NDC], F32, kind="ExternalInput").ap()
    w_out = nc.dram_tensor("w_out", [D, H], F32, kind="ExternalInput").ap()
    w1_t = nc.dram_tensor("w1_t", [32, 16, 128, 128], F32, kind="ExternalInput").ap()
    b1_t = nc.dram_tensor("b1_t", [128, 32], F32, kind="ExternalInput").ap()
    w2_t = nc.dram_tensor("w2_t", [32, 32, 128, 128], F32, kind="ExternalInput").ap()
    b2_t = nc.dram_tensor("b2_t", [128, 32], F32, kind="ExternalInput").ap()
    w3t = nc.dram_tensor("w3t", [FF, H], F32, kind="ExternalInput").ap()
    b3_rep = nc.dram_tensor("b3_rep", [128, H], F32, kind="ExternalInput").ap()
    eye_in = nc.dram_tensor("eye_in", [128, 128], F32, kind="ExternalInput").ap()
    ones_in = nc.dram_tensor("ones_in", [1, 128], F32, kind="ExternalInput").ap()
    cmask = nc.dram_tensor("cmask", [128, NCORES], F32, kind="ExternalInput").ap()
    out = nc.dram_tensor("out", [T, H], F32, kind="ExternalOutput").ap()

    sp_zs = nc.dram_tensor("sp_zs", [NDC, 128, T], BF16).ap()
    sp_c = nc.dram_tensor("sp_c", [NDC, 128, T], F32).ap()
    sp_y0 = nc.dram_tensor("sp_y0", [NDC, 128, T], BF16).ap()
    sp_y = nc.dram_tensor("sp_y", [NDC, 128, T], BF16).ap()
    sp_out1 = nc.dram_tensor("sp_out1", [NTC, 128, H], F32).ap()
    cc_in = nc.dram_tensor("cc_in", [256, 512], F32)
    cc_out = nc.dram_tensor("cc_out", [256 * NCORES, 512], F32)

    sv = bs.sem("sv")
    ss = bs.sem("ss")
    stt = bs.sem("st")
    sg = bs.sem("sg")
    scc = bs.sem("scc")
    s_const = DStream(bs, "sK", 1)
    s_x = DStream(bs, "sX", 3)
    s_tr = DStream(bs, "sT", 1)
    s_trc = DStream(bs, "sU", 1)
    s_zsp = DStream(bs, "sZ", 2)
    s_csp = DStream(bs, "sC", 2)
    s_y0sp = DStream(bs, "sY", 1)
    s_ysp = DStream(bs, "sW", 2)
    s_xres = DStream(bs, "sR", 2)
    s_o1sp = DStream(bs, "sO", 1)
    s_o1r = DStream(bs, "sQ", 2)
    s_out = DStream(bs, "sV", 2)
    s_bcf = DStream(bs, "sF", 1)
    g_win = DStream(bs, "gA", 4)
    g_conv = DStream(bs, "gB", 1)
    g_wx = DStream(bs, "gC", 2)
    g_wdt = DStream(bs, "gD", 2)
    g_cc = DStream(bs, "gE", 1)
    g_p7 = DStream(bs, "gF", 2)
    g_wo = DStream(bs, "gG", 4)
    g_w1 = DStream(bs, "gH", 2)
    g_w2 = DStream(bs, "gI", 2)
    g_w3 = DStream(bs, "gJ", 4)
    s_w0 = DStream(bs, "sw0", 2)
    s_w2b = DStream(bs, "sw2", 2)
    c_wdt = DStream(bs, "cwd", 2)

    block = nc.Block().__enter__()

    _COMPUTE = {"tensor_tensor", "tensor_scalar", "tensor_copy", "tensor_tensor_scan",
                "tensor_add", "tensor_sub", "tensor_mul", "tensor_max", "tensor_relu",
                "reduce_sum", "reduce_max", "tensor_reduce", "reciprocal", "activation",
                "memset", "memzero", "scalar_tensor_tensor", "select", "copy", "add",
                "mul", "sqrt", "square", "sign", "tensor_single_scalar",
                "tensor_scalar_add", "tensor_scalar_sub", "tensor_scalar_mul",
                "tensor_scalar_max", "tensor_scalar_min"}

    class EW:

        def __init__(self, eng, sem, tag):
            self._eng = eng
            self._sem = sem
            self._tag = tag

        def __getattr__(self, name):
            fn = getattr(self._eng, name)
            if (SIM_SAFE or self._tag == "vector") and name in _COMPUTE and self._sem is not None:
                def wrapped(*a, **k):
                    bs.wait(self._eng, {self._sem: bs.cnt[self._sem.name]}, self._tag)
                    return fn(*a, **k)
                return wrapped
            return fn

    _ESEM = {"vector": sv, "scalar": ss, "gpsimd": None, "tensor": None, "sync": None}

    _REC = {k: [] for k in ("sync", "vector", "scalar", "tensor", "gpsimd")}

    class FakeInstr:
        __slots__ = ("incs",)

        def __init__(self):
            self.incs = []

        def then_inc(self, sem, amt=1):
            self.incs.append((sem, amt))
            return self

    class Rec:
        def __init__(self, tag):
            self._tag = tag

        def __getattr__(self, name):
            def call(*a, **k):
                fi = FakeInstr()
                _REC[self._tag].append((name, a, k, fi))
                return fi
            return call

    def on(engname, fn):
        fn(EW(Rec(engname), _ESEM.get(engname), engname))

    def flush_streams():
        for engname, rec in _REC.items():
            def replay(eng, rec=rec):
                for name, a, k, fi in rec:
                    inst = getattr(eng, name)(*a, **k)
                    for sem, amt in fi.incs:
                        inst.then_inc(sem, amt)
            getattr(block, engname)(replay)

    def barrier():
        deps = {s: bs.cnt[s.name] for s in bs.all_sems}
        for e in ("sync", "vector", "scalar", "tensor", "gpsimd"):
            on(e, lambda eng, deps=deps, e=e: bs.wait(eng, deps, e))

    def row(sp, j, n=T):
        return bass.AP(sp.tensor, j * 128 * n, [[n, 128], [1, n]])

    def mmn(tensor, psl, lhsT, rhs, start, stop, width):
        i = None
        for o in range(0, width, 512):
            w = min(512, width - o)
            i = tensor.matmul(psl[:, o:o + w], lhsT, rhs[:, o:o + w], start=start, stop=stop)
        return i

    pers = [nc.sbuf_tensor("A_sb", [128, NDC * S], F32),
            nc.sbuf_tensor("bdt_sb", [128, NDC], F32),
            nc.sbuf_tensor("cvb_sb", [128, NDC], F32),
            nc.sbuf_tensor("D_sb", [128, NDC], F32),
            nc.sbuf_tensor("b1_sb", [128, 32], F32),
            nc.sbuf_tensor("b2_sb", [128, 32], F32),
            nc.sbuf_tensor("eye_sb", [128, 128], BF16),
            nc.sbuf_tensor("ones_sb", [1, 128], BF16),
            nc.sbuf_tensor("mask_sb", [128, NCORES], F32),
            nc.sbuf_tensor("zcol_sb", [128, 1], F32),
            nc.sbuf_tensor("eps_sb", [128, 1], F32),
            nc.sbuf_tensor("ctot_sb", [128, NDC], F32),
            nc.sbuf_tensor("L_sb", [128, NDC * S], F32),
            nc.sbuf_tensor("hinit_sb", [128, NDC * S], F32),
            nc.sbuf_tensor("dtin_sb", [128, T], BF16)]
    (A_sb, bdt_sb, cvb_sb, D_sb, b1_sb, b2_sb, eye_sb, ones_sb, mask_sb,
     zcol_sb, eps_sb, ctot_sb, L_sb, hinit_sb, dtin_sb) = [p.__enter__() for p in pers]

    psum = nc.psum_tensor("ps", [128, 4096], F32).__enter__()

    with nc.sbuf_tensor("t0f", [128, NDC * S], F32) as t0f, \
         nc.sbuf_tensor("t0e", [128, 128], F32) as t0e, \
         nc.sbuf_tensor("t0o", [1, 128], F32) as t0o:

        def p0_sync(sync):
            for j in range(NDC):
                s_const.dma(sync, 0, t0f[:, j * S:(j + 1) * S],
                            a_log[j * 128:(j + 1) * 128, :])
            for dst, srcp in ((bdt_sb, b_dt_t), (cvb_sb, conv_b_t), (D_sb, d_t),
                              (b1_sb, b1_t), (b2_sb, b2_t), (mask_sb, cmask)):
                s_const.dma(sync, 0, dst[:], srcp[:, :])
            s_const.dma(sync, 0, t0e[:], eye_in[:, :])
            s_const.dma(sync, 0, t0o[:], ones_in[:, :])
        on("sync", p0_sync)
        d0 = s_const.snap(0)

        def p0_se(scalar):
            bs.wait(scalar, d0, "scalar")
            bs.inc(scalar.activation(A_sb[:], t0f[:], ACT.Exp, bias=0.0, scale=1.0), ss)
        on("scalar", p0_se)
        s0 = bs.snap(ss)

        def p0_ve(vector):
            bs.wait(vector, d0, "vector")
            bs.wait(vector, s0, "vector")
            bs.inc(vector.tensor_scalar(A_sb[:], A_sb[:], -1.0, None, ALU.mult), sv)
            bs.inc(vector.tensor_copy(eye_sb[:], t0e[:]), sv)
            bs.inc(vector.tensor_copy(ones_sb[:], t0o[:]), sv)
            vector.memset(zcol_sb[:], 0.0)
            vector.memset(eps_sb[:], EPS)
            bs.inc(vector.nop(), sv)
        on("vector", p0_ve)
        barrier()

    xc_sc = nc.sbuf_tensor("xc_sb", [128, NDC * T], BF16)
    xc_sb = xc_sc.__enter__()
    lnx_sc = nc.sbuf_tensor("lnxfm", [128, NHC * TP], BF16)
    lnxfm = lnx_sc.__enter__()

    with nc.sbuf_tensor("xtm", [128, 3 * H], F32) as xtm, \
         nc.sbuf_tensor("lnxtm", [128, 9 * H], BF16) as lnxtm, \
         nc.sbuf_tensor("sq1", [128, H], F32) as sq1, \
         nc.sbuf_tensor("st8", [128, 8], F32) as st8:
        ln_done = []
        ve_done = []
        for ti in range(9):
            buf = xtm[:, (ti % 3) * H:(ti % 3) * H + H]

            def p1_ld(sync, ti=ti, buf=buf):
                if ti >= 3:
                    bs.wait(sync, ln_done[ti - 3], "sync")
                s_x.dma(sync, ti, buf, x_pad[ti * 128:(ti + 1) * 128, :])
            on("sync", p1_ld)
            dld = s_x.snap(ti)

            def p1_se1(scalar, ti=ti, buf=buf):
                bs.wait(scalar, dld, "scalar")
                if ti >= 1:
                    bs.wait(scalar, ve_done[ti - 1], "scalar")
                bs.inc(scalar.activation(sq1[:], buf, ACT.Square, bias=0.0, scale=1.0), ss)
            on("scalar", p1_se1)
            sse = bs.snap(ss)

            def p1_ve(vector, buf=buf):
                bs.wait(vector, dld, "vector")
                bs.inc(vector.reduce_sum(st8[:, 0:1], buf, mybir.AxisListType.X), sv)
                bs.wait(vector, sse, "vector")
                bs.inc(vector.reduce_sum(st8[:, 1:2], sq1[:], mybir.AxisListType.X), sv)
                bs.inc(vector.tensor_scalar(st8[:, 2:3], st8[:, 0:1], 1.0 / H, None, ALU.mult), sv)
                bs.inc(vector.tensor_scalar(st8[:, 3:4], st8[:, 1:2], 1.0 / H, None, ALU.mult), sv)
                bs.inc(vector.tensor_tensor(st8[:, 4:5], st8[:, 2:3], st8[:, 2:3], ALU.mult), sv)
                bs.inc(vector.tensor_tensor(st8[:, 5:6], st8[:, 3:4], st8[:, 4:5], ALU.subtract), sv)
            on("vector", p1_ve)
            sve = bs.snap(sv)
            ve_done.append(sve)

            def p1_se2(scalar):
                bs.wait(scalar, sve, "scalar")
                bs.inc(scalar.activation(st8[:, 6:7], st8[:, 5:6], ACT.Sqrt, bias=eps_sb[:, 0:1], scale=1.0), ss)
            on("scalar", p1_se2)
            sse2 = bs.snap(ss)

            def p1_ve2(vector, ti=ti, buf=buf):
                bs.wait(vector, sse2, "vector")
                bs.inc(vector.reciprocal(st8[:, 7:8], st8[:, 6:7]), sv)
                bs.inc(vector.tensor_scalar(lnxtm[:, ti * H:(ti + 1) * H], buf,
                                            st8[:, 2:3], st8[:, 7:8], ALU.subtract, ALU.mult), sv)
            on("vector", p1_ve2)
            ln_done.append(bs.snap(sv, ss))

        trd = bs.snap(sv)
        trs = [(hc, ti) for hc in range(NHC) for ti in range(9)]
        for qi, qn in enumerate(["sync", "scalar"]):
            part = trs[qi::2]
            strm = {"sync": s_tr, "scalar": s_trc}[qn]

            def p1_trq(eng, part=part, qn=qn, strm=strm):
                bs.wait(eng, trd, qn)
                for hc, ti in part:
                    bs.inc(eng.dma_start_transpose(
                        lnxfm[:, hc * TP + ti * 128: hc * TP + (ti + 1) * 128],
                        lnxtm[:, ti * H + hc * 128: ti * H + hc * 128 + 128]),
                        strm.sems[0], 16)
            on(qn, p1_trq)
        barrier()

    with nc.sbuf_tensor("wstg", [128, 4 * H], F32) as wstg, \
         nc.sbuf_tensor("wtl", [128, 4 * H], BF16) as wtl, \
         nc.sbuf_tensor("xpt", [128, 2 * 1026], BF16) as xpt, \
         nc.sbuf_tensor("cstg", [128, 384], F32) as cstg, \
         nc.sbuf_tensor("ctl", [128, 384], BF16) as ctl, \
         nc.sbuf_tensor("zev", [128, 2 * T], BF16) as zev:
        cast_done = [None] * 64
        mm_done = [None] * 64
        ev_done = [None] * 64
        cv_mm = [None] * 32
        cv_done = [None] * 32
        ccast_done = [None] * 32
        zsp = [None] * 32
        for m in range(64):
            is_xp = m < 32
            wb = m % 4

            def p2_gld(gpsimd, m=m, wb=wb):
                if m >= 4:
                    bs.wait(gpsimd, mm_done[m - 4], "gpsimd")
                for q in range(4):
                    g_win.dma(gpsimd, m,
                              bass.AP(wstg.ap().tensor, wb * H + q * 512,
                                      [[4 * H, 128], [128, 4], [1, 128]]),
                              bass.AP(w_in_t.tensor, (m * 16 + q * 4) * 128 * 128,
                                      [[128, 128], [128 * 128, 4], [1, 128]]))
            on("gpsimd", p2_gld)
            dW = g_win.snap(m)

            def p2_cast(vector, m=m, wb=wb):
                bs.wait(vector, dW, "vector")
                if m >= 4:
                    bs.wait(vector, mm_done[m - 4], "vector")
                bs.inc(vector.tensor_copy(wtl[:, wb * H:wb * H + H], wstg[:, wb * H:wb * H + H]), sv)
            on("vector", p2_cast)
            cast_done[m] = bs.snap(sv)

            pslice = psum[:, (m % 2) * 1536:(m % 2) * 1536 + 1026]

            def p2_mm(tensor, m=m, is_xp=is_xp, wb=wb, pslice=pslice):
                bs.wait(tensor, cast_done[m], "tensor")
                if m >= 2:
                    bs.wait(tensor, ev_done[m - 2], "tensor")
                i2 = None
                for k in range(16):
                    lhsT = wtl[:, wb * H + k * 128: wb * H + (k + 1) * 128]
                    if is_xp:
                        i2 = mmn(tensor, pslice, lhsT,
                                 bass.AP(lnxfm.ap().tensor, k * TP,
                                         [[NHC * TP, 128], [1, 1026]]),
                                 (k == 0), (k == 15), 1026)
                    else:
                        i2 = mmn(tensor, pslice, lhsT,
                                 bass.AP(lnxfm.ap().tensor, k * TP + 1,
                                         [[NHC * TP, 128], [1, 1024]]),
                                 (k == 0), (k == 15), 1024)
                bs.inc(i2, stt)
            on("tensor", p2_mm)
            mm_done[m] = bs.snap(stt)

            if is_xp:
                j = m
                xslice = xpt[:, (j % 2) * 1026:(j % 2) * 1026 + 1026]

                def p2_ev(vector, j=j, xslice=xslice, pslice=pslice):
                    bs.wait(vector, mm_done[j], "vector")
                    if j >= 2:
                        bs.wait(vector, cv_mm[j - 2], "vector")
                    bs.inc(vector.tensor_copy(xslice, pslice), sv)
                on("vector", p2_ev)
                ev_done[m] = bs.snap(sv)

                def p2_cld(gpsimd, j=j):
                    if j >= 1:
                        bs.wait(gpsimd, ccast_done[j - 1], "gpsimd")
                    g_conv.dma(gpsimd, 0, cstg[:, 0:384],
                               bass.AP(conv_diag.tensor, j * 3 * 128 * 128,
                                       [[128, 128], [128 * 128, 3], [1, 128]]))
                on("gpsimd", p2_cld)
                dC = g_conv.snap(0)

                def p2_ccast(vector, j=j):
                    bs.wait(vector, dC, "vector")
                    if j >= 1:
                        bs.wait(vector, cv_mm[j - 1], "vector")
                    bs.inc(vector.tensor_copy(ctl[:, 0:384], cstg[:, 0:384]), sv)
                on("vector", p2_ccast)
                ccast_done[j] = bs.snap(sv)

                cps = psum[:, 3072:4096]

                def p2_conv(tensor, j=j, xslice=xslice, cps=cps):
                    bs.wait(tensor, ccast_done[j], "tensor")
                    bs.wait(tensor, ev_done[j], "tensor")
                    if j >= 1:
                        bs.wait(tensor, cv_done[j - 1], "tensor")
                    i = None
                    for tap in range(3):
                        i = mmn(tensor, cps, ctl[:, tap * 128:(tap + 1) * 128],
                                xslice[:, tap:tap + 1024],
                                (tap == 0), (tap == 2), 1024)
                    bs.inc(i, stt)
                on("tensor", p2_conv)
                cv_mm[j] = bs.snap(stt)

                def p2_silu(scalar, j=j, cps=cps):
                    bs.wait(scalar, cv_mm[j], "scalar")
                    bs.inc(scalar.activation(xc_sb[:, j * T:(j + 1) * T], cps,
                                             ACT.Silu, bias=cvb_sb[:, j:j + 1], scale=1.0), ss)
                on("scalar", p2_silu)
                cv_done[j] = bs.snap(ss)

            else:
                j = m - 32
                zslice = zev[:, (j % 2) * T:(j % 2) * T + T]

                def p2_zev(scalar, j=j, m=m, pslice=pslice, zslice=zslice):
                    bs.wait(scalar, mm_done[m], "scalar")
                    if j >= 2:
                        bs.wait(scalar, zsp[j - 2], "scalar")
                    bs.inc(scalar.activation(zslice, pslice[:, 0:1024],
                                             ACT.Silu, bias=0.0, scale=1.0), ss)
                on("scalar", p2_zev)
                ev_done[m] = bs.snap(ss)

                def p2_zsp(sync, j=j, m=m, zslice=zslice):
                    bs.wait(sync, ev_done[m], "sync")
                    s_zsp.dma(sync, j, row(sp_zs, j), zslice)
                on("sync", p2_zsp)
                zsp[j] = s_zsp.snap(j)
        barrier()
    lnx_sc.__exit__(None, None, None)

    Crep_sc = nc.sbuf_tensor("Crep", [128, S * T], BF16)
    Brep_sc = nc.sbuf_tensor("Brep", [128, S * T], BF16)
    Crep = Crep_sc.__enter__()
    Brep = Brep_sc.__enter__()

    with nc.sbuf_tensor("wxs", [128, 2 * 160], F32) as wxs, \
         nc.sbuf_tensor("wxb", [128, 4 * 160], BF16) as wxb, \
         nc.sbuf_tensor("bcs", [32, T], F32) as bcs, \
         nc.sbuf_tensor("bcb", [32, T], BF16) as bcb, \
         nc.sbuf_tensor("bcf", [1, S * T], BF16) as bcf:
        wx_cast = [None] * NDC
        wx_mm = [None] * NDC
        for j in range(NDC):
            def p3_ld(gpsimd, j=j):
                if j >= 2:
                    bs.wait(gpsimd, wx_cast[j - 2], "gpsimd")
                g_wx.dma(gpsimd, j, wxs[:, (j % 2) * 160:(j % 2) * 160 + 160],
                         w_x[j * 128:(j + 1) * 128, :])
            on("gpsimd", p3_ld)
            dwx = g_wx.snap(j)

            def p3_cast(vector, j=j):
                bs.wait(vector, dwx, "vector")
                if j >= 4:
                    bs.wait(vector, wx_mm[j - 4], "vector")
                bs.inc(vector.tensor_copy(wxb[:, (j % 4) * 160:(j % 4) * 160 + 160],
                                          wxs[:, (j % 2) * 160:(j % 2) * 160 + 160]), sv)
            on("vector", p3_cast)
            wx_cast[j] = bs.snap(sv)

            def p3_mm(tensor, j=j):
                bs.wait(tensor, wx_cast[j], "tensor")
                lt = wxb[:, (j % 4) * 160:(j % 4) * 160 + 160]
                mmn(tensor, psum[:, 0:1024], lt[:, 0:128], xc_sb[:, j * T:(j + 1) * T],
                    (j == 0), (j == NDC - 1), 1024)
                i = mmn(tensor, psum[0:32, 1024:2048], lt[:, 128:160],
                        xc_sb[:, j * T:(j + 1) * T],
                        (j == 0), (j == NDC - 1), 1024)
                bs.inc(i, stt)
            on("tensor", p3_mm)
            wx_mm[j] = bs.snap(stt)
        mmx = bs.snap(stt)

        def p3_ev(vector):
            bs.wait(vector, mmx, "vector")
            bs.inc(vector.tensor_copy(dtin_sb[:], psum[:, 0:1024]), sv)
            bs.inc(vector.tensor_copy(bcs[:, :], psum[0:32, 1024:2048]), sv)
        on("vector", p3_ev)
        sev = bs.snap(sv)

        def p3_cast2(vector):
            bs.wait(vector, sev, "vector")
            bs.inc(vector.tensor_copy(bcb[:, :], bcs[:, :]), sv)
        on("vector", p3_cast2)
        sbc = bs.snap(sv)

        last_rep = None
        rep_ev = [None] * 32
        for half, dst in ((0, Brep), (1, Crep)):
            def p3_fl(sync, half=half, last_rep=last_rep):
                bs.wait(sync, sbc, "sync")
                if last_rep is not None:
                    bs.wait(sync, last_rep, "sync")
                s_bcf.dma(sync, 0, bcf[0:1, :], bcb[half * 16:half * 16 + 16, 0:T])
            on("sync", p3_fl)
            dfl = s_bcf.snap(0)

            for q in range(16):
                qb = q % 2
                gq = half * 16 + q

                def p3_rep(tensor, q=q, qb=qb, gq=gq, dfl=dfl):
                    bs.wait(tensor, dfl, "tensor")
                    if gq >= 2:
                        bs.wait(tensor, rep_ev[gq - 2], "tensor")
                    bs.inc(mmn(tensor, psum[:, 2048 + qb * 1024: 2048 + qb * 1024 + 1024],
                               ones_sb[0:1, :], bcf[0:1, q * 1024:(q + 1) * 1024],
                               True, True, 1024), stt)
                on("tensor", p3_rep)
                mr = bs.snap(stt)

                def p3_rev(vector, q=q, qb=qb, dst=dst, mr=mr):
                    bs.wait(vector, mr, "vector")
                    bs.inc(vector.tensor_copy(dst[:, q * 1024:(q + 1) * 1024],
                                              psum[:, 2048 + qb * 1024: 2048 + qb * 1024 + 1024]), sv)
                on("vector", p3_rev)
                rep_ev[gq] = bs.snap(sv)
            last_rep = bs.snap(stt)
        barrier()

    with nc.sbuf_tensor("wdts", [128, 2 * 128], F32) as wdts, \
         nc.sbuf_tensor("wdtb", [128, 2 * 128], BF16) as wdtb, \
         nc.sbuf_tensor("dtj", [128, 2 * T], F32) as dtj, \
         nc.sbuf_tensor("cj", [128, 2 * T], F32) as cj, \
         nc.sbuf_tensor("vj", [128, T], BF16) as vj, \
         nc.sbuf_tensor("xcDt", [128, T], BF16) as xcDt, \
         nc.sbuf_tensor("abuf", [128, 12 * T], BF16) as abuf, \
         nc.sbuf_tensor("bbuf", [128, T], BF16) as bbuf, \
         nc.sbuf_tensor("hbuf", [128, T], BF16) as hbuf, \
         nc.sbuf_tensor("hCbuf", [128, 4 * T], BF16) as hCbuf, \
         nc.sbuf_tensor("y0ev", [128, T], BF16) as y0ev:
        vj_done = [None] * NDC
        mmreds = [None] * NDC
        wc_done = [None] * NDC
        y0sp = [None] * NDC
        csp = [None] * NDC
        dtev = [None] * NDC
        scan_done = {}
        a_done = {}
        L_done = {}
        hC_done = {}
        red_done = {}
        y0ev_done = [None] * NDC

        def prev_key(j, s, k=4):
            if s - k >= 0:
                return (j, s - k)
            if j - 1 >= 0:
                return (j - 1, S - k + s)
            return None

        def emit_dthead(j):
            jb = j % 2
            def p5_ld(gpsimd, j=j, jb=jb):
                if j >= 2:
                    bs.wait(gpsimd, wc_done[j - 2], "gpsimd")
                g_wdt.dma(gpsimd, j, wdts[:, jb * 128:jb * 128 + 128],
                          bass.AP(w_dt_t.tensor, j * 128 * 128, [[128, 128], [1, 128]]))
            on("gpsimd", p5_ld)
            dxc = g_wdt.snap(j)

            def p5_wcast(vector, j=j, jb=jb):
                bs.wait(vector, dxc, "vector")
                bs.inc(vector.tensor_copy(wdtb[:, jb * 128:jb * 128 + 128],
                                          wdts[:, jb * 128:jb * 128 + 128]), sv)
            on("vector", p5_wcast)
            wc = bs.snap(sv)
            wc_done[j] = wc

            dtps = psum[:, 2048 + jb * 1024: 2048 + jb * 1024 + 1024]

            def p5_dtmm(tensor, j=j, dtps=dtps):
                bs.wait(tensor, wc, "tensor")
                if j >= 2:
                    bs.wait(tensor, dtev[j - 2], "tensor")
                bs.inc(mmn(tensor, dtps, wdtb[:, (j % 2) * 128:(j % 2) * 128 + 128],
                           dtin_sb[:, :], True, True, 1024), stt)
            on("tensor", p5_dtmm)
            mdt = bs.snap(stt)

            return dtps, mdt

        def emit_dttail(j, dtps, mdt):
            jb = j % 2
            def p5_sp1(scalar, j=j, jb=jb, dtps=dtps):
                bs.wait(scalar, mdt, "scalar")
                if j >= 2:
                    bs.wait(scalar, vj_done[j - 2], "scalar")
                    bs.wait(scalar, a_done[(j - 2, S - 1)], "scalar")
                bs.inc(scalar.activation(dtj[:, jb * T:jb * T + T], dtps, ACT.Exp,
                                         bias=bdt_sb[:, j:j + 1], scale=1.0), ss)
            on("scalar", p5_sp1)
            sexp = bs.snap(ss)

            def p5_sp2(vector, j=j, jb=jb, sexp=sexp):
                bs.wait(vector, sexp, "vector")
                bs.inc(vector.tensor_scalar(dtj[:, jb * T:jb * T + T],
                                            dtj[:, jb * T:jb * T + T], 1.0, None, ALU.add), sv)
            on("vector", p5_sp2)
            vp1 = bs.snap(sv)

            def p5_sp3(scalar, j=j, jb=jb, vp1=vp1):
                bs.wait(scalar, vp1, "scalar")
                bs.inc(scalar.activation(dtj[:, jb * T:jb * T + T], dtj[:, jb * T:jb * T + T],
                                         ACT.Ln, bias=0.0, scale=1.0), ss)
            on("scalar", p5_sp3)
            ssp = bs.snap(ss)
            dtev[j] = ssp

            def p5_cscan(vector, j=j, jb=jb):
                bs.wait(vector, ssp, "vector")
                if j >= 2:
                    bs.wait(vector, csp[j - 2], "vector")
                zrow = bass.AP(zcol_sb.ap().tensor, 0, [[1, 128], [0, T]])
                bs.inc(vector.tensor_tensor_scan(cj[:, jb * T:jb * T + T],
                                                 dtj[:, jb * T:jb * T + T], zrow,
                                                 zcol_sb[:, 0:1], ALU.add, ALU.add), sv)
                bs.inc(vector.tensor_copy(ctot_sb[:, j:j + 1],
                                          cj[:, jb * T + T - 1: jb * T + T]), sv)
                if j >= 1 and mmreds[j - 1] is not None:
                    bs.wait(vector, mmreds[j - 1], "vector")
                bs.inc(vector.tensor_tensor(vj[:, :], dtj[:, jb * T:jb * T + T],
                                            xc_sb[:, j * T:(j + 1) * T], ALU.mult), sv)
                bs.inc(vector.tensor_scalar(xcDt[:, :], xc_sb[:, j * T:(j + 1) * T],
                                            D_sb[:, j:j + 1], None, ALU.mult), sv)
            on("vector", p5_cscan)
            vj_done[j] = bs.snap(sv)

            def p5_csp(sync, j=j, jb=jb):
                bs.wait(sync, vj_done[j], "sync")
                s_csp.dma(sync, j, row(sp_c, j), cj[:, jb * T:jb * T + T])
            on("sync", p5_csp)
            csp[j] = s_csp.snap(j)


        def emit_sloop(j):
            jb = j % 2
            ssp = dtev[j]
            for s in range(S):
                sb = s % 4

                def p5_a(scalar, j=j, jb=jb, s=s):
                    bs.wait(scalar, ssp, "scalar")
                    pka = prev_key(j, s, 12)
                    if pka is not None and pka in scan_done:
                        bs.wait(scalar, scan_done[pka], "scalar")
                    gi = (j * S + s) % 12
                    bs.inc(scalar.activation(abuf[:, gi * T:gi * T + T],
                                             dtj[:, jb * T:jb * T + T], ACT.Exp,
                                             bias=0.0, scale=A_sb[:, j * S + s:j * S + s + 1]), ss)
                on("scalar", p5_a)
                a_done[(j, s)] = bs.snap(ss)

                def p5_scan(vector, j=j, jb=jb, s=s, sb=sb):
                    bs.wait(vector, a_done[(j, s)], "vector")
                    pk = prev_key(j, s)
                    if pk is not None and pk in red_done:
                        bs.wait(vector, red_done[pk], "vector")
                    bs.inc(vector.tensor_tensor(bbuf[:, :], vj[:, :],
                                                Brep[:, s * T:(s + 1) * T], ALU.mult), sv)
                    gi = (j * S + s) % 12
                    bs.inc(vector.tensor_tensor_scan(hbuf[:, 0:T],
                                                     abuf[:, gi * T:gi * T + T],
                                                     bbuf[:, :],
                                                     zcol_sb[:, 0:1], ALU.mult, ALU.add), sv)
                    bs.inc(vector.tensor_copy(L_sb[:, j * S + s:j * S + s + 1],
                                              hbuf[:, T - 1: T]), sv)
                    bs.inc(vector.tensor_tensor(hCbuf[:, sb * T:sb * T + T],
                                                hbuf[:, 0:T],
                                                Crep[:, s * T:(s + 1) * T], ALU.mult), sv)
                on("vector", p5_scan)
                scan_done[(j, s)] = bs.snap(sv)
                L_done[(j, s)] = scan_done[(j, s)]

                def p5_red(tensor, j=j, jb=jb, s=s, sb=sb):
                    bs.wait(tensor, scan_done[(j, s)], "tensor")
                    if s == 0 and j >= 2:
                        bs.wait(tensor, y0ev_done[j - 2], "tensor")
                    bs.inc(mmn(tensor, psum[:, jb * 1024: jb * 1024 + 1024], eye_sb[:, :],
                               hCbuf[:, sb * T:sb * T + T],
                               (s == 0), False, 1024), stt)
                on("tensor", p5_red)
                red_done[(j, s)] = bs.snap(stt)

            def p5_redD(tensor, j=j, jb=jb):
                bs.wait(tensor, vj_done[j], "tensor")
                bs.inc(mmn(tensor, psum[:, jb * 1024: jb * 1024 + 1024], eye_sb[:, :],
                           xcDt[:, :], False, True, 1024), stt)
            on("tensor", p5_redD)
            mmred = bs.snap(stt)
            mmreds[j] = mmred

            def p5_y0ev(vector, j=j, jb=jb, mmred=mmred):
                bs.wait(vector, mmred, "vector")
                if j >= 1:
                    bs.wait(vector, y0sp[j - 1], "vector")
                bs.inc(vector.tensor_copy(y0ev[:, :], psum[:, jb * 1024: jb * 1024 + 1024]), sv)
            on("vector", p5_y0ev)
            y0ev_done[j] = bs.snap(sv)

            def p5_y0sp(sync, j=j):
                bs.wait(sync, y0ev_done[j], "sync")
                s_y0sp.dma(sync, 0, row(sp_y0, j), y0ev[:, :])
            on("sync", p5_y0sp)
            y0sp[j] = s_y0sp.snap(0)

        dtps0, mdt0 = emit_dthead(0)
        emit_dttail(0, dtps0, mdt0)
        pend = None
        for j in range(NDC):
            if j + 1 < NDC:
                pend = emit_dthead(j + 1)
            emit_sloop(j)
            if j + 1 < NDC:
                emit_dttail(j + 1, *pend)
        barrier()
    Brep_sc.__exit__(None, None, None)

    with nc.sbuf_tensor("argP", [128, NDC * S], F32) as argP, \
         nc.sbuf_tensor("Ptot", [128, NDC * S], F32) as Ptot, \
         nc.sbuf_tensor("Lk", [128, NCORES * 512], F32) as Lk, \
         nc.sbuf_tensor("Pk", [128, NCORES * 512], F32) as Pk, \
         nc.sbuf_tensor("run6", [128, 512], F32) as run6, \
         nc.sbuf_tensor("tm6", [128, 512], F32) as tm6:

        def p6_arg(vector):
            ctot_b = bass.AP(ctot_sb.ap().tensor, ctot_sb.ap().offset,
                             [[NDC, 128], [1, NDC], [0, S]])
            bs.inc(vector.tensor_tensor(argP[:, :], ctot_b, A_sb[:, :], ALU.mult), sv)
        on("vector", p6_arg)
        va = bs.snap(sv)

        def p6_exp(scalar):
            bs.wait(scalar, va, "scalar")
            bs.inc(scalar.activation(Ptot[:, :], argP[:, :], ACT.Exp, bias=0.0, scale=1.0), ss)
        on("scalar", p6_exp)
        se6 = bs.snap(ss)

        def p6_cc(gpsimd):
            bs.wait(gpsimd, se6, "gpsimd")
            g_cc.dma(gpsimd, 0, cc_in.ap()[0:128, :], L_sb[:, :])
            g_cc.dma(gpsimd, 0, cc_in.ap()[128:256, :], Ptot[:, :])
            bs.wait(gpsimd, g_cc.snap(0), "gpsimd")
            bs.inc(gpsimd.collective_compute(
                "AllGather", ALU.bypass, replica_groups=[list(range(NCORES))],
                ins=[cc_in.ap().opt()], outs=[cc_out.ap().opt()]), scc, 1)
            gpsimd.wait_ge(scc, bs.cnt["scc"])
            for k in range(NCORES):
                g_cc.dma(gpsimd, 0, Lk[:, k * 512:(k + 1) * 512],
                         cc_out.ap()[k * 256:k * 256 + 128, :])
                g_cc.dma(gpsimd, 0, Pk[:, k * 512:(k + 1) * 512],
                         cc_out.ap()[k * 256 + 128:(k + 1) * 256, :])
        on("gpsimd", p6_cc)
        d6 = {**g_cc.snap(0), scc: bs.cnt["scc"]}

        def p6_pref(vector):
            bs.wait(vector, d6, "vector")
            vector.memset(hinit_sb[:, :], 0.0)
            bs.inc(vector.tensor_copy(run6[:, :], Lk[:, 0:512]), sv)
            for k in range(1, NCORES):
                bs.inc(vector.tensor_scalar(tm6[:, :], run6[:, :],
                                            mask_sb[:, k:k + 1], None, ALU.mult), sv)
                bs.inc(vector.tensor_tensor(hinit_sb[:, :], hinit_sb[:, :], tm6[:, :], ALU.add), sv)
                if k < NCORES - 1:
                    bs.inc(vector.tensor_tensor(tm6[:, :], Pk[:, k * 512:(k + 1) * 512],
                                                run6[:, :], ALU.mult), sv)
                    bs.inc(vector.tensor_tensor(run6[:, :], tm6[:, :],
                                                Lk[:, k * 512:(k + 1) * 512], ALU.add), sv)
        on("vector", p6_pref)
        barrier()

    with nc.sbuf_tensor("cj7", [128, 2 * T], F32) as cj7, \
         nc.sbuf_tensor("zs7", [128, 2 * T], BF16) as zs7, \
         nc.sbuf_tensor("y07", [128, 2 * T], BF16) as y07, \
         nc.sbuf_tensor("Pb7", [128, S * T], BF16) as Pb7, \
         nc.sbuf_tensor("co7", [128, 4 * T], BF16) as co7, \
         nc.sbuf_tensor("yC7", [128, 4 * T], BF16) as yC7, \
         nc.sbuf_tensor("yt7", [128, T], F32) as yt7, \
         nc.sbuf_tensor("ye7", [128, 2 * T], BF16) as ye7:
        ld7 = [None] * NDC
        P7 = {}
        yC7d = {}
        red7d = {}
        yev7 = [None] * NDC
        ysp7 = [None] * NDC

        def pk7(j, s, k=4):
            if s - k >= 0:
                return (j, s - k)
            if j - 1 >= 0:
                return (j - 1, S - k + s)
            return None

        for j in range(NDC):
            jb = j % 2

            def p7_ld(gpsimd, j=j, jb=jb):
                if j >= 2:
                    bs.wait(gpsimd, yev7[j - 2], "gpsimd")
                g_p7.dma(gpsimd, j, cj7[:, jb * T:jb * T + T], row(sp_c, j))
                g_p7.dma(gpsimd, j, zs7[:, jb * T:jb * T + T], row(sp_zs, j))
                g_p7.dma(gpsimd, j, y07[:, jb * T:jb * T + T], row(sp_y0, j))
            on("gpsimd", p7_ld)
            ld7[j] = g_p7.snap(j)

            for s in range(S):
                sb = s % 4

                def p7_P(scalar, j=j, jb=jb, s=s):
                    bs.wait(scalar, ld7[j], "scalar")
                    if j >= 1 and (j - 1, s) in yC7d:
                        bs.wait(scalar, yC7d[(j - 1, s)], "scalar")
                    bs.inc(scalar.activation(Pb7[:, s * T:s * T + T],
                                             cj7[:, jb * T:jb * T + T], ACT.Exp,
                                             bias=0.0, scale=A_sb[:, j * S + s:j * S + s + 1]), ss)
                on("scalar", p7_P)
                P7[(j, s)] = bs.snap(ss)

                def p7_y1(vector, j=j, s=s, sb=sb):
                    bs.wait(vector, P7[(j, s)], "vector")
                    pk = pk7(j, s)
                    if pk is not None and pk in red7d:
                        bs.wait(vector, red7d[pk], "vector")
                    bs.inc(vector.tensor_scalar(co7[:, sb * T:sb * T + T],
                                                Pb7[:, s * T:s * T + T],
                                                hinit_sb[:, j * S + s:j * S + s + 1],
                                                None, ALU.mult), sv)
                    bs.inc(vector.tensor_tensor(yC7[:, sb * T:sb * T + T],
                                                co7[:, sb * T:sb * T + T],
                                                Crep[:, s * T:(s + 1) * T], ALU.mult), sv)
                on("vector", p7_y1)
                yC7d[(j, s)] = bs.snap(sv)

                def p7_red(tensor, j=j, jb=jb, s=s, sb=sb):
                    bs.wait(tensor, yC7d[(j, s)], "tensor")
                    if s == 0 and j >= 2:
                        bs.wait(tensor, yev7[j - 2], "tensor")
                    bs.inc(mmn(tensor, psum[:, jb * 1024: jb * 1024 + 1024], eye_sb[:, :],
                               yC7[:, sb * T:sb * T + T],
                               (s == 0), (s == S - 1), 1024), stt)
                on("tensor", p7_red)
                red7d[(j, s)] = bs.snap(stt)
            red7 = bs.snap(stt)

            def p7_y(vector, j=j, jb=jb, red7=red7):
                bs.wait(vector, red7, "vector")
                bs.wait(vector, ld7[j], "vector")
                if j >= 2:
                    bs.wait(vector, ysp7[j - 2], "vector")
                bs.inc(vector.tensor_tensor(yt7[:, :], y07[:, jb * T:jb * T + T],
                                            psum[:, jb * 1024: jb * 1024 + 1024], ALU.add), sv)
                bs.inc(vector.tensor_tensor(ye7[:, jb * T:jb * T + T], yt7[:, :],
                                            zs7[:, jb * T:jb * T + T], ALU.mult), sv)
            on("vector", p7_y)
            yev7[j] = bs.snap(sv)

            def p7_ysp(sync, j=j, jb=jb):
                bs.wait(sync, yev7[j], "sync")
                s_ysp.dma(sync, j, row(sp_y, j), ye7[:, jb * T:jb * T + T])
            on("sync", p7_ysp)
            ysp7[j] = s_ysp.snap(j)
        barrier()
    Crep_sc.__exit__(None, None, None)
    xc_sc.__exit__(None, None, None)

    lnx2_sc = nc.sbuf_tensor("lnx2fm", [128, NHC * T], BF16)
    lnx2fm = lnx2_sc.__enter__()
    out1_sc = nc.sbuf_tensor("out1", [128, NTC * H], F32)
    out1 = out1_sc.__enter__()

    with nc.sbuf_tensor("ych", [128, 4 * T], BF16) as ych, \
         nc.sbuf_tensor("wos", [128, 4 * 512], F32) as wos, \
         nc.sbuf_tensor("wob", [128, 4 * 512], BF16) as wob, \
         nc.sbuf_tensor("xres", [128, 2 * 512], F32) as xres:
        evhq = [None] * 4
        ycast = {}
        mm7b = {}
        evts = {}
        for hq in range(4):
            for j in range(NDC):
                ib = (hq * NDC + j) % 4

                def p7b_ld(gpsimd, hq=hq, j=j, ib=ib):
                    key = hq * NDC + j - 4
                    if key >= 0:
                        bs.wait(gpsimd, mm7b[key], "gpsimd")
                    g_wo.dma(gpsimd, hq * NDC + j, ych[:, ib * T:ib * T + T], row(sp_y, j))
                    g_wo.dma(gpsimd, hq * NDC + j,
                             wos[:, ib * 512:ib * 512 + 512],
                             w_out[j * 128:(j + 1) * 128, hq * 512:(hq + 1) * 512])
                on("gpsimd", p7b_ld)
                dwo = g_wo.snap(hq * NDC + j)

                def p7b_cast(vector, hq=hq, j=j, ib=ib, dwo=dwo):
                    bs.wait(vector, dwo, "vector")
                    key = hq * NDC + j - 4
                    if key >= 0:
                        bs.wait(vector, mm7b[key], "vector")
                    bs.inc(vector.tensor_copy(wob[:, ib * 512:ib * 512 + 512],
                                              wos[:, ib * 512:ib * 512 + 512]), sv)
                on("vector", p7b_cast)
                ycast[hq * NDC + j] = bs.snap(sv)

                def p7b_mm(tensor, hq=hq, j=j, ib=ib):
                    bs.wait(tensor, ycast[hq * NDC + j], "tensor")
                    if j == 0 and hq >= 1:
                        bs.wait(tensor, evhq[hq - 1], "tensor")
                    i = None
                    for ts in range(NTC):
                        i = tensor.matmul(psum[:, ts * 512:(ts + 1) * 512],
                                          ych[:, ib * T + ts * 128: ib * T + (ts + 1) * 128],
                                          wob[:, ib * 512:ib * 512 + 512],
                                          start=(j == 0), stop=(j == NDC - 1))
                    bs.inc(i, stt)
                on("tensor", p7b_mm)
                mm7b[hq * NDC + j] = bs.snap(stt)
            mmw = bs.snap(stt)

            for ts in range(NTC):
                def p7b_xld(sync, hq=hq, ts=ts):
                    if (hq * NTC + ts) >= 2:
                        pts = (hq * NTC + ts) - 2
                        bs.wait(sync, evts[(pts // NTC, pts % NTC)], "sync")
                    s_xres.dma(sync, hq * NTC + ts,
                               xres[:, (ts % 2) * 512:(ts % 2) * 512 + 512],
                               x_pad[1 + ts * 128: 1 + (ts + 1) * 128,
                                     hq * 512:(hq + 1) * 512])
                on("sync", p7b_xld)
                dx = s_xres.snap(hq * NTC + ts)

                def p7b_ev(vector, hq=hq, ts=ts, mmw=mmw, dx=dx):
                    bs.wait(vector, mmw, "vector")
                    bs.wait(vector, dx, "vector")
                    bs.inc(vector.tensor_tensor(
                        out1[:, ts * H + hq * 512: ts * H + hq * 512 + 512],
                        xres[:, (ts % 2) * 512:(ts % 2) * 512 + 512],
                        psum[:, ts * 512:(ts + 1) * 512], ALU.add), sv)
                on("vector", p7b_ev)
                evts[(hq, ts)] = bs.snap(sv)
            evhq[hq] = bs.snap(sv)
        barrier()

        with nc.sbuf_tensor("ln2tm", [128, NTC * H], BF16) as ln2tm, \
             nc.sbuf_tensor("sq8", [128, H], F32) as sq8, \
             nc.sbuf_tensor("st88", [128, 8], F32) as st88:
            vprev = None
            for ts in range(NTC):
                tile = out1[:, ts * H:(ts + 1) * H]

                def p8_se1(scalar, tile=tile, vprev=vprev):
                    if vprev is not None:
                        bs.wait(scalar, vprev, "scalar")
                    bs.inc(scalar.activation(sq8[:], tile, ACT.Square, bias=0.0, scale=1.0), ss)
                on("scalar", p8_se1)
                s81 = bs.snap(ss)

                def p8_ve(vector, tile=tile, s81=s81):
                    bs.inc(vector.reduce_sum(st88[:, 0:1], tile, mybir.AxisListType.X), sv)
                    bs.wait(vector, s81, "vector")
                    bs.inc(vector.reduce_sum(st88[:, 1:2], sq8[:], mybir.AxisListType.X), sv)
                    bs.inc(vector.tensor_scalar(st88[:, 2:3], st88[:, 0:1], 1.0 / H, None, ALU.mult), sv)
                    bs.inc(vector.tensor_scalar(st88[:, 3:4], st88[:, 1:2], 1.0 / H, None, ALU.mult), sv)
                    bs.inc(vector.tensor_tensor(st88[:, 4:5], st88[:, 2:3], st88[:, 2:3], ALU.mult), sv)
                    bs.inc(vector.tensor_tensor(st88[:, 5:6], st88[:, 3:4], st88[:, 4:5], ALU.subtract), sv)
                on("vector", p8_ve)
                s82 = bs.snap(sv)
                vprev = s82

                def p8_se2(scalar, s82=s82):
                    bs.wait(scalar, s82, "scalar")
                    bs.inc(scalar.activation(st88[:, 6:7], st88[:, 5:6], ACT.Sqrt, bias=eps_sb[:, 0:1], scale=1.0), ss)
                on("scalar", p8_se2)
                s83 = bs.snap(ss)

                def p8_ve2(vector, ts=ts, tile=tile, s83=s83):
                    bs.wait(vector, s83, "vector")
                    bs.inc(vector.reciprocal(st88[:, 7:8], st88[:, 6:7]), sv)
                    bs.inc(vector.tensor_scalar(ln2tm[:, ts * H:(ts + 1) * H], tile,
                                                st88[:, 2:3], st88[:, 7:8],
                                                ALU.subtract, ALU.mult), sv)
                on("vector", p8_ve2)
                s84 = bs.snap(sv)
                vprev = s84

                def p8_osp(sync, ts=ts, tile=tile, s84=s84):
                    bs.wait(sync, s84, "sync")
                    s_o1sp.dma(sync, 0, bass.AP(sp_out1.tensor, ts * 128 * H,
                                                [[H, 128], [1, H]]), tile)
                on("sync", p8_osp)

            tr8 = bs.snap(sv)
            trs8 = [(hc, ts) for hc in range(NHC) for ts in range(NTC)]
            for qi, qn in enumerate(["sync", "scalar"]):
                part = trs8[qi::2]
                strm = {"sync": s_tr, "scalar": s_trc}[qn]

                def p8_trq(eng, part=part, qn=qn, strm=strm):
                    bs.wait(eng, tr8, qn)
                    for hc, ts in part:
                        bs.inc(eng.dma_start_transpose(
                            lnx2fm[:, hc * T + ts * 128: hc * T + (ts + 1) * 128],
                            ln2tm[:, ts * H + hc * 128: ts * H + hc * 128 + 128]),
                            strm.sems[0], 16)
                on(qn, p8_trq)
            barrier()
    out1_sc.__exit__(None, None, None)

    h1_sc = nc.sbuf_tensor("h1", [128, 32 * T], BF16)
    h1 = h1_sc.__enter__()
    h2_sc = nc.sbuf_tensor("h2", [128, 32 * T], BF16)
    h2 = h2_sc.__enter__()
    with nc.sbuf_tensor("w9s", [128, 2 * 1024], F32) as w9s, \
         nc.sbuf_tensor("w9b", [128, 2 * H], BF16) as w9b:
        c9 = [None] * 32
        c9r = [None] * 64
        m9s = [None] * 32
        e9 = [None] * 32
        for m in range(32):
            mb = m % 2

            for r in range(2):
                rb = r % 2

                def p9_ld(gpsimd, m=m, r=r, rb=rb):
                    if m * 2 + r >= 2:
                        bs.wait(gpsimd, c9r[m * 2 + r - 2], "gpsimd")
                    for q in range(2):
                        g_w1.dma(gpsimd, m * 2 + r,
                                 bass.AP(w9s.ap().tensor, rb * 1024 + q * 512,
                                         [[2 * 1024, 128], [128, 4], [1, 128]]),
                                 bass.AP(w1_t.tensor, (m * 16 + r * 8 + q * 4) * 128 * 128,
                                         [[128, 128], [128 * 128, 4], [1, 128]]))
                on("gpsimd", p9_ld)
                d9 = g_w1.snap(m * 2 + r)

                def p9_cast(vector, m=m, mb=mb, r=r, rb=rb, d9=d9):
                    bs.wait(vector, d9, "vector")
                    if m >= 2 and r == 0:
                        bs.wait(vector, m9s[m - 2], "vector")
                    bs.inc(vector.tensor_copy(w9b[:, mb * H + r * 1024: mb * H + r * 1024 + 1024],
                                              w9s[:, rb * 1024: rb * 1024 + 1024]), sv)
                on("vector", p9_cast)
                c9r[m * 2 + r] = bs.snap(sv)
            c9[m] = bs.snap(sv)

            def p9_mm(tensor, m=m, mb=mb):
                bs.wait(tensor, c9[m], "tensor")
                if m >= 2:
                    bs.wait(tensor, e9[m - 2], "tensor")
                i = None
                for k in range(16):
                    i = mmn(tensor, psum[:, mb * 1024: mb * 1024 + 1024],
                            w9b[:, mb * H + k * 128: mb * H + (k + 1) * 128],
                            lnx2fm[:, k * T:(k + 1) * T],
                            (k == 0), (k == 15), 1024)
                bs.inc(i, stt)
            on("tensor", p9_mm)
            m9 = bs.snap(stt)
            m9s[m] = m9

            def p9_ev(scalar, m=m, mb=mb, m9=m9):
                bs.wait(scalar, m9, "scalar")
                bs.inc(scalar.activation(h1[:, m * T:(m + 1) * T],
                                         psum[:, mb * 1024: mb * 1024 + 1024],
                                         ACT.Relu, bias=b1_sb[:, m:m + 1], scale=1.0), ss)
            on("scalar", p9_ev)
            e9[m] = bs.snap(ss)
        barrier()

    with nc.sbuf_tensor("was9", [128, 2 * 1024], F32) as was9, \
         nc.sbuf_tensor("wab9", [128, 2 * FF], BF16) as wab9:
        ca = [None] * 32
        car = [None] * 128
        mas = [None] * 32
        ea = [None] * 32
        for m in range(32):
            mb = m % 2

            for r in range(4):
                rb = r % 2

                strm3, qn3 = (g_w2, "gpsimd") if r < 2 else (s_w2b, "sync")

                def pa_ld(eng, m=m, r=r, rb=rb, strm3=strm3, qn3=qn3):
                    if m * 4 + r >= 2:
                        bs.wait(eng, car[m * 4 + r - 2], qn3)
                    for q in range(2):
                        strm3.dma(eng, m * 2 + (r % 2),
                                 bass.AP(was9.ap().tensor, rb * 1024 + q * 512,
                                         [[2 * 1024, 128], [128, 4], [1, 128]]),
                                 bass.AP(w2_t.tensor, (m * 32 + r * 8 + q * 4) * 128 * 128,
                                         [[128, 128], [128 * 128, 4], [1, 128]]))
                on(qn3, pa_ld)
                da = strm3.snap(m * 2 + (r % 2))

                def pa_cast(vector, m=m, mb=mb, r=r, rb=rb, da=da):
                    bs.wait(vector, da, "vector")
                    if m >= 2 and r == 0:
                        bs.wait(vector, mas[m - 2], "vector")
                    bs.inc(vector.tensor_copy(
                        wab9[:, mb * FF + r * 1024: mb * FF + r * 1024 + 1024],
                        was9[:, rb * 1024: rb * 1024 + 1024]), sv)
                on("vector", pa_cast)
                car[m * 4 + r] = bs.snap(sv)
            ca[m] = bs.snap(sv)

            def pa_mm(tensor, m=m, mb=mb):
                bs.wait(tensor, ca[m], "tensor")
                if m >= 2:
                    bs.wait(tensor, ea[m - 2], "tensor")
                i = None
                for k in range(32):
                    i = mmn(tensor, psum[:, mb * 1024: mb * 1024 + 1024],
                            wab9[:, mb * FF + k * 128: mb * FF + (k + 1) * 128],
                            h1[:, k * T:(k + 1) * T],
                            (k == 0), (k == 31), 1024)
                bs.inc(i, stt)
            on("tensor", pa_mm)
            ma = bs.snap(stt)
            mas[m] = ma

            def pa_ev(scalar, m=m, mb=mb, ma=ma):
                bs.wait(scalar, ma, "scalar")
                bs.inc(scalar.activation(h2[:, m * T:(m + 1) * T],
                                         psum[:, mb * 1024: mb * 1024 + 1024],
                                         ACT.Relu, bias=b2_sb[:, m:m + 1], scale=1.0), ss)
            on("scalar", pa_ev)
            ea[m] = bs.snap(ss)
        barrier()

    with nc.sbuf_tensor("b3r_sb", [128, H], F32) as b3r_sb, \
         nc.sbuf_tensor("w3s", [128, 4 * 512], F32) as w3s, \
         nc.sbuf_tensor("w3b", [128, 4 * 512], BF16) as w3b, \
         nc.sbuf_tensor("o1r", [128, 2 * 512], F32) as o1r, \
         nc.sbuf_tensor("fin", [128, 2 * 512], F32) as fin:

        def pb_b3(sync):
            s_const.dma(sync, 0, b3r_sb[:], b3_rep[:, :])
        on("sync", pb_b3)
        db3 = s_const.snap(0)

        evqs = [None] * 4
        stts = {}
        fev_d = {}
        cb = {}
        mm3d = {}
        for hq in range(4):
            for k in range(32):
                ib = (hq * 32 + k) % 4

                def pb_ld(gpsimd, hq=hq, k=k, ib=ib):
                    key = hq * 32 + k - 4
                    if key >= 0:
                        bs.wait(gpsimd, cb[key], "gpsimd")
                    if key >= 0:
                        bs.wait(gpsimd, mm3d[key], "gpsimd")
                    g_w3.dma(gpsimd, hq * 32 + k,
                             w3s[:, ib * 512:ib * 512 + 512],
                             w3t[k * 128:(k + 1) * 128, hq * 512:(hq + 1) * 512])
                on("gpsimd", pb_ld)
                dw3 = g_w3.snap(hq * 32 + k)

                def pb_cast(vector, hq=hq, k=k, ib=ib, dw3=dw3):
                    bs.wait(vector, dw3, "vector")
                    key = hq * 32 + k - 4
                    if key >= 0:
                        bs.wait(vector, mm3d[key], "vector")
                    bs.inc(vector.tensor_copy(w3b[:, ib * 512:ib * 512 + 512],
                                              w3s[:, ib * 512:ib * 512 + 512]), sv)
                on("vector", pb_cast)
                cb[hq * 32 + k] = bs.snap(sv)

                def pb_mm(tensor, hq=hq, k=k, ib=ib):
                    bs.wait(tensor, cb[hq * 32 + k], "tensor")
                    if k == 0 and hq >= 1:
                        bs.wait(tensor, evqs[hq - 1], "tensor")
                    i = None
                    for ts in range(NTC):
                        i = tensor.matmul(psum[:, ts * 512:(ts + 1) * 512],
                                          h2[:, k * T + ts * 128: k * T + (ts + 1) * 128],
                                          w3b[:, ib * 512:ib * 512 + 512],
                                          start=(k == 0), stop=(k == 31))
                    bs.inc(i, stt)
                on("tensor", pb_mm)
                mm3d[hq * 32 + k] = bs.snap(stt)
            m3 = bs.snap(stt)

            for ts in range(NTC):
                def pb_o1(sync, hq=hq, ts=ts):
                    if (hq * NTC + ts) >= 2:
                        bs.wait(sync, fev_d.get(hq * NTC + ts - 2, {}), "sync")
                    s_o1r.dma(sync, hq * NTC + ts,
                              o1r[:, (ts % 2) * 512:(ts % 2) * 512 + 512],
                              bass.AP(sp_out1.tensor, ts * 128 * H + hq * 512,
                                      [[H, 128], [1, 512]]))
                on("sync", pb_o1)
                do1 = s_o1r.snap(hq * NTC + ts)

                def pb_ev(vector, hq=hq, ts=ts, m3=m3, do1=do1):
                    bs.wait(vector, m3, "vector")
                    bs.wait(vector, do1, "vector")
                    bs.wait(vector, db3, "vector")
                    if (hq * NTC + ts) >= 2:
                        bs.wait(vector, s_out.snap(hq * NTC + ts - 2), "vector")
                    sl = fin[:, (ts % 2) * 512:(ts % 2) * 512 + 512]
                    bs.inc(vector.tensor_tensor(sl, psum[:, ts * 512:(ts + 1) * 512],
                                                b3r_sb[:, hq * 512:(hq + 1) * 512], ALU.add), sv)
                    bs.inc(vector.tensor_tensor(sl, sl,
                                                o1r[:, (ts % 2) * 512:(ts % 2) * 512 + 512],
                                                ALU.add), sv)
                on("vector", pb_ev)
                fev = bs.snap(sv)
                fev_d[hq * NTC + ts] = fev

                def pb_st(sync, hq=hq, ts=ts, fev=fev):
                    bs.wait(sync, fev, "sync")
                    s_out.dma(sync, hq * NTC + ts,
                              out[ts * 128:(ts + 1) * 128, hq * 512:(hq + 1) * 512],
                              fin[:, (ts % 2) * 512:(ts % 2) * 512 + 512])
                on("sync", pb_st)
                stts[(hq, ts)] = s_out.snap(hq * NTC + ts)
            evqs[hq] = bs.snap(sv)

        fdeps = s_out.snap_all()

        def fin_wait(gpsimd):
            bs.wait(gpsimd, fdeps, "gpsimd")
        on("gpsimd", fin_wait)
        flush_streams()
    h2_sc.__exit__(None, None, None)
    h1_sc.__exit__(None, None, None)
    lnx2_sc.__exit__(None, None, None)

    block.__exit__(None, None, None)
    return nc


_NC_CACHE = {}


def _prep(inputs):
    W_in = np.ascontiguousarray(np.asarray(inputs["L0_attn_W_in"], np.float32))
    w_in_tt = np.ascontiguousarray(W_in.reshape(16, 128, 64, 128).transpose(2, 0, 1, 3))
    conv_w = np.asarray(inputs["L0_attn_conv_w"], np.float32)[:, 0, :]
    cd = np.zeros((NDC, 3, 128, 128), np.float32)
    idx = np.arange(128)
    for j in range(NDC):
        for tap in range(3):
            cd[j, tap, idx, idx] = conv_w[j * 128:(j + 1) * 128, tap]

    def cols(v, n):
        return np.ascontiguousarray(np.asarray(v, np.float32).reshape(n, 128).T)

    shared = {
        "w_in_t": w_in_tt,
        "conv_diag": cd,
        "conv_b_t": cols(inputs["L0_attn_conv_b"], 32),
        "w_x": np.ascontiguousarray(np.asarray(inputs["L0_attn_W_x"], np.float32)),
        "w_dt_t": np.ascontiguousarray(
            np.asarray(inputs["L0_attn_W_dt"], np.float32).reshape(128, 32, 128).transpose(1, 0, 2)),
        "b_dt_t": cols(inputs["L0_attn_b_dt"], 32),
        "a_log": np.ascontiguousarray(np.asarray(inputs["L0_attn_A_log"], np.float32)),
        "d_t": cols(inputs["L0_attn_D"], 32),
        "w_out": np.ascontiguousarray(np.asarray(inputs["L0_attn_W_out"], np.float32)),
        "w1_t": np.ascontiguousarray(
            np.asarray(inputs["L0_ffn_W1"], np.float32).T.reshape(16, 128, 32, 128).transpose(2, 0, 1, 3)),
        "b1_t": cols(inputs["L0_ffn_b1"], 32),
        "w2_t": np.ascontiguousarray(
            np.asarray(inputs["L0_ffn_W2"], np.float32).T.reshape(32, 128, 32, 128).transpose(2, 0, 1, 3)),
        "b2_t": cols(inputs["L0_ffn_b2"], 32),
        "w3t": np.ascontiguousarray(np.asarray(inputs["L0_ffn_W3"], np.float32).T),
        "b3_rep": np.ascontiguousarray(
            np.tile(np.asarray(inputs["L0_ffn_b3"], np.float32)[None, :], (128, 1))),
        "eye_in": np.eye(128, dtype=np.float32),
        "ones_in": np.ones((1, 128), np.float32),
    }
    x = np.ascontiguousarray(np.asarray(inputs["L0_attn_x"], np.float32)[0])
    in_maps = []
    for c in range(NCORES):
        xp = np.zeros((TP, H), np.float32)
        xp[1:1 + T] = x[c * T:(c + 1) * T]
        if c > 0:
            xp[0] = x[c * T - 1]
        if c < NCORES - 1:
            xp[1 + T] = x[(c + 1) * T]
        cm = np.zeros((128, NCORES), np.float32)
        cm[:, c] = 1.0
        im = dict(shared)
        im["x_pad"] = xp
        im["cmask"] = cm
        in_maps.append(im)
    return in_maps


def kernel(**inputs):
    in_maps = _prep(inputs)
    if "nc" not in _NC_CACHE:
        _NC_CACHE["nc"] = build()
    res = run_bass_kernel_spmd(_NC_CACHE["nc"], in_maps, core_ids=list(range(NCORES)))
    o = np.concatenate([r["out"] for r in res.results], axis=0)
    return o.reshape(1, 8192, 2048).astype(np.float32)


# version 27
# speedup vs baseline: 1.1194x; 1.1194x over previous
import numpy as np
import concourse.bass as bass
import concourse.mybir as mybir
from concourse.bass_utils import run_bass_kernel_spmd

SIM_SAFE = False

F32 = mybir.dt.float32
BF16 = mybir.dt.bfloat16
ALU = mybir.AluOpType
ACT = mybir.ActivationFunctionType

NCORES = 8
T = 1024
TP = 1152
D = 4096
S = 16
H = 2048
FF = 4096
NDC = 32
NHC = 16
NTC = 8
EPS = 1e-5


class Bld:
    def __init__(self, nc):
        self.nc = nc
        self.cnt = {}
        self.guar = {}

    def sem(self, name):
        s = self.nc.semaphore(name).__enter__()
        self.cnt[name] = 0
        self.all_sems = getattr(self, "all_sems", [])
        self.all_sems.append(s)
        return s

    def inc(self, instr, sem, amt=1):
        instr.then_inc(sem, amt)
        self.cnt[sem.name] += amt
        return instr

    def snap(self, *sems):
        return {s: self.cnt[s.name] for s in sems}

    def wait(self, eng, deps, tag):
        if not deps:
            return
        g = self.guar.setdefault(tag, {})
        for sem, v in deps.items():
            if v > 0 and g.get(sem.name, 0) < v:
                eng.wait_ge(sem, v)
                g[sem.name] = v


class DStream:

    def __init__(self, bs, name, nslots):
        self.bs = bs
        self.sems = [bs.sem(f"{name}{i}") for i in range(nslots)]
        self.n = nslots

    def dma(self, eng, slot, out, in_, **kw):
        s = self.sems[slot % self.n]
        self.bs.inc(eng.dma_start(out=out, in_=in_, **kw), s, 16)

    def snap(self, slot):
        s = self.sems[slot % self.n]
        return {s: self.bs.cnt[s.name]}

    def snap_all(self):
        return {s: self.bs.cnt[s.name] for s in self.sems}


def build():
    nc = bass.Bass("TRN2", target_bir_lowering=False,
                   detect_race_conditions=SIM_SAFE)
    bs = Bld(nc)

    x_pad = nc.dram_tensor("x_pad", [TP, H], F32, kind="ExternalInput").ap()
    w_in_t = nc.dram_tensor("w_in_t", [64, 16, 128, 128], F32, kind="ExternalInput").ap()
    conv_diag = nc.dram_tensor("conv_diag", [NDC, 3, 128, 128], F32, kind="ExternalInput").ap()
    conv_b_t = nc.dram_tensor("conv_b_t", [128, NDC], F32, kind="ExternalInput").ap()
    w_x = nc.dram_tensor("w_x", [D, 160], F32, kind="ExternalInput").ap()
    w_dt_t = nc.dram_tensor("w_dt_t", [NDC, 128, 128], F32, kind="ExternalInput").ap()
    b_dt_t = nc.dram_tensor("b_dt_t", [128, NDC], F32, kind="ExternalInput").ap()
    a_log = nc.dram_tensor("a_log", [D, S], F32, kind="ExternalInput").ap()
    d_t = nc.dram_tensor("d_t", [128, NDC], F32, kind="ExternalInput").ap()
    w_out = nc.dram_tensor("w_out", [D, H], F32, kind="ExternalInput").ap()
    w1_t = nc.dram_tensor("w1_t", [32, 16, 128, 128], F32, kind="ExternalInput").ap()
    b1_t = nc.dram_tensor("b1_t", [128, 32], F32, kind="ExternalInput").ap()
    w2_t = nc.dram_tensor("w2_t", [32, 32, 128, 128], F32, kind="ExternalInput").ap()
    b2_t = nc.dram_tensor("b2_t", [128, 32], F32, kind="ExternalInput").ap()
    w3t = nc.dram_tensor("w3t", [FF, H], F32, kind="ExternalInput").ap()
    b3_rep = nc.dram_tensor("b3_rep", [128, H], F32, kind="ExternalInput").ap()
    eye_in = nc.dram_tensor("eye_in", [128, 128], F32, kind="ExternalInput").ap()
    ones_in = nc.dram_tensor("ones_in", [1, 128], F32, kind="ExternalInput").ap()
    cmask = nc.dram_tensor("cmask", [128, NCORES], F32, kind="ExternalInput").ap()
    out = nc.dram_tensor("out", [T, H], F32, kind="ExternalOutput").ap()

    sp_zs = nc.dram_tensor("sp_zs", [NDC, 128, T], BF16).ap()
    sp_c = nc.dram_tensor("sp_c", [NDC, 128, T], F32).ap()
    sp_y0 = nc.dram_tensor("sp_y0", [NDC, 128, T], BF16).ap()
    sp_y = nc.dram_tensor("sp_y", [NDC, 128, T], BF16).ap()
    sp_out1 = nc.dram_tensor("sp_out1", [NTC, 128, H], F32).ap()
    cc_in = nc.dram_tensor("cc_in", [256, 512], F32)
    cc_out = nc.dram_tensor("cc_out", [256 * NCORES, 512], F32)

    sv = bs.sem("sv")
    ss = bs.sem("ss")
    stt = bs.sem("st")
    sg = bs.sem("sg")
    scc = bs.sem("scc")
    s_const = DStream(bs, "sK", 1)
    s_x = DStream(bs, "sX", 3)
    s_tr = DStream(bs, "sT", 1)
    s_trc = DStream(bs, "sU", 1)
    s_zsp = DStream(bs, "sZ", 2)
    s_csp = DStream(bs, "sC", 2)
    s_y0sp = DStream(bs, "sY", 2)
    s_ysp = DStream(bs, "sW", 2)
    s_xres = DStream(bs, "sR", 2)
    s_o1sp = DStream(bs, "sO", 1)
    s_o1r = DStream(bs, "sQ", 2)
    s_out = DStream(bs, "sV", 2)
    s_bcf = DStream(bs, "sF", 1)
    g_win = DStream(bs, "gA", 2)
    g_conv = DStream(bs, "gB", 1)
    g_wx = DStream(bs, "gC", 2)
    g_wdt = DStream(bs, "gD", 2)
    g_cc = DStream(bs, "gE", 1)
    g_p7 = DStream(bs, "gF", 2)
    g_wo = DStream(bs, "gG", 4)
    g_w1 = DStream(bs, "gH", 2)
    g_w2 = DStream(bs, "gI", 2)
    g_w3 = DStream(bs, "gJ", 4)
    s_w0 = DStream(bs, "sw0", 2)
    s_w2b = DStream(bs, "sw2", 2)
    c_wdt = DStream(bs, "cwd", 2)

    block = nc.Block().__enter__()

    _COMPUTE = {"tensor_tensor", "tensor_scalar", "tensor_copy", "tensor_tensor_scan",
                "tensor_add", "tensor_sub", "tensor_mul", "tensor_max", "tensor_relu",
                "reduce_sum", "reduce_max", "tensor_reduce", "reciprocal", "activation",
                "memset", "memzero", "scalar_tensor_tensor", "select", "copy", "add",
                "mul", "sqrt", "square", "sign", "tensor_single_scalar",
                "tensor_scalar_add", "tensor_scalar_sub", "tensor_scalar_mul",
                "tensor_scalar_max", "tensor_scalar_min"}

    class EW:

        def __init__(self, eng, sem, tag):
            self._eng = eng
            self._sem = sem
            self._tag = tag

        def __getattr__(self, name):
            fn = getattr(self._eng, name)
            if (SIM_SAFE or self._tag == "vector") and name in _COMPUTE and self._sem is not None:
                def wrapped(*a, **k):
                    bs.wait(self._eng, {self._sem: bs.cnt[self._sem.name]}, self._tag)
                    return fn(*a, **k)
                return wrapped
            return fn

    _ESEM = {"vector": sv, "scalar": ss, "gpsimd": None, "tensor": None, "sync": None}

    _REC = {k: [] for k in ("sync", "vector", "scalar", "tensor", "gpsimd")}

    class FakeInstr:
        __slots__ = ("incs",)

        def __init__(self):
            self.incs = []

        def then_inc(self, sem, amt=1):
            self.incs.append((sem, amt))
            return self

    class Rec:
        def __init__(self, tag):
            self._tag = tag

        def __getattr__(self, name):
            def call(*a, **k):
                fi = FakeInstr()
                _REC[self._tag].append((name, a, k, fi))
                return fi
            return call

    def on(engname, fn):
        fn(EW(Rec(engname), _ESEM.get(engname), engname))

    def flush_streams():
        for engname, rec in _REC.items():
            def replay(eng, rec=rec):
                for name, a, k, fi in rec:
                    inst = getattr(eng, name)(*a, **k)
                    for sem, amt in fi.incs:
                        inst.then_inc(sem, amt)
            getattr(block, engname)(replay)

    def barrier():
        deps = {s: bs.cnt[s.name] for s in bs.all_sems}
        for e in ("sync", "vector", "scalar", "tensor", "gpsimd"):
            on(e, lambda eng, deps=deps, e=e: bs.wait(eng, deps, e))

    def row(sp, j, n=T):
        return bass.AP(sp.tensor, j * 128 * n, [[n, 128], [1, n]])

    def mmn(tensor, psl, lhsT, rhs, start, stop, width):
        i = None
        for o in range(0, width, 512):
            w = min(512, width - o)
            i = tensor.matmul(psl[:, o:o + w], lhsT, rhs[:, o:o + w], start=start, stop=stop)
        return i

    pers = [nc.sbuf_tensor("A_sb", [128, NDC * S], F32),
            nc.sbuf_tensor("bdt_sb", [128, NDC], F32),
            nc.sbuf_tensor("cvb_sb", [128, NDC], F32),
            nc.sbuf_tensor("D_sb", [128, NDC], F32),
            nc.sbuf_tensor("b1_sb", [128, 32], F32),
            nc.sbuf_tensor("b2_sb", [128, 32], F32),
            nc.sbuf_tensor("eye_sb", [128, 128], BF16),
            nc.sbuf_tensor("ones_sb", [1, 128], BF16),
            nc.sbuf_tensor("mask_sb", [128, NCORES], F32),
            nc.sbuf_tensor("zcol_sb", [128, 1], F32),
            nc.sbuf_tensor("eps_sb", [128, 1], F32),
            nc.sbuf_tensor("ctot_sb", [128, NDC], F32),
            nc.sbuf_tensor("L_sb", [128, NDC * S], F32),
            nc.sbuf_tensor("hinit_sb", [128, NDC * S], F32),
            nc.sbuf_tensor("dtin_sb", [128, T], BF16)]
    (A_sb, bdt_sb, cvb_sb, D_sb, b1_sb, b2_sb, eye_sb, ones_sb, mask_sb,
     zcol_sb, eps_sb, ctot_sb, L_sb, hinit_sb, dtin_sb) = [p.__enter__() for p in pers]

    psum = nc.psum_tensor("ps", [128, 4096], F32).__enter__()

    with nc.sbuf_tensor("t0f", [128, NDC * S], F32) as t0f, \
         nc.sbuf_tensor("t0e", [128, 128], F32) as t0e, \
         nc.sbuf_tensor("t0o", [1, 128], F32) as t0o:

        def p0_sync(sync):
            for j in range(NDC):
                s_const.dma(sync, 0, t0f[:, j * S:(j + 1) * S],
                            a_log[j * 128:(j + 1) * 128, :])
            for dst, srcp in ((bdt_sb, b_dt_t), (cvb_sb, conv_b_t), (D_sb, d_t),
                              (b1_sb, b1_t), (b2_sb, b2_t), (mask_sb, cmask)):
                s_const.dma(sync, 0, dst[:], srcp[:, :])
            s_const.dma(sync, 0, t0e[:], eye_in[:, :])
            s_const.dma(sync, 0, t0o[:], ones_in[:, :])
        on("sync", p0_sync)
        d0 = s_const.snap(0)

        def p0_se(scalar):
            bs.wait(scalar, d0, "scalar")
            bs.inc(scalar.activation(A_sb[:], t0f[:], ACT.Exp, bias=0.0, scale=1.0), ss)
        on("scalar", p0_se)
        s0 = bs.snap(ss)

        def p0_ve(vector):
            bs.wait(vector, d0, "vector")
            bs.wait(vector, s0, "vector")
            bs.inc(vector.tensor_scalar(A_sb[:], A_sb[:], -1.0, None, ALU.mult), sv)
            bs.inc(vector.tensor_copy(eye_sb[:], t0e[:]), sv)
            bs.inc(vector.tensor_copy(ones_sb[:], t0o[:]), sv)
            vector.memset(zcol_sb[:], 0.0)
            vector.memset(eps_sb[:], EPS)
            bs.inc(vector.nop(), sv)
        on("vector", p0_ve)
        barrier()

    xc_sc = nc.sbuf_tensor("xc_sb", [128, NDC * T], BF16)
    xc_sb = xc_sc.__enter__()
    lnx_sc = nc.sbuf_tensor("lnxfm", [128, NHC * TP], BF16)
    lnxfm = lnx_sc.__enter__()

    with nc.sbuf_tensor("xtm", [128, 3 * H], F32) as xtm, \
         nc.sbuf_tensor("lnxtm", [128, 9 * H], BF16) as lnxtm, \
         nc.sbuf_tensor("sq1", [128, H], F32) as sq1, \
         nc.sbuf_tensor("st8", [128, 8], F32) as st8:
        ln_done = []
        ve_done = []
        for ti in range(9):
            buf = xtm[:, (ti % 3) * H:(ti % 3) * H + H]

            def p1_ld(sync, ti=ti, buf=buf):
                if ti >= 3:
                    bs.wait(sync, ln_done[ti - 3], "sync")
                s_x.dma(sync, ti, buf, x_pad[ti * 128:(ti + 1) * 128, :])
            on("sync", p1_ld)
            dld = s_x.snap(ti)

            def p1_se1(scalar, ti=ti, buf=buf):
                bs.wait(scalar, dld, "scalar")
                if ti >= 1:
                    bs.wait(scalar, ve_done[ti - 1], "scalar")
                bs.inc(scalar.activation(sq1[:], buf, ACT.Square, bias=0.0, scale=1.0), ss)
            on("scalar", p1_se1)
            sse = bs.snap(ss)

            def p1_ve(vector, buf=buf):
                bs.wait(vector, dld, "vector")
                bs.inc(vector.reduce_sum(st8[:, 0:1], buf, mybir.AxisListType.X), sv)
                bs.wait(vector, sse, "vector")
                bs.inc(vector.reduce_sum(st8[:, 1:2], sq1[:], mybir.AxisListType.X), sv)
                bs.inc(vector.tensor_scalar(st8[:, 2:3], st8[:, 0:1], 1.0 / H, None, ALU.mult), sv)
                bs.inc(vector.tensor_scalar(st8[:, 3:4], st8[:, 1:2], 1.0 / H, None, ALU.mult), sv)
                bs.inc(vector.tensor_tensor(st8[:, 4:5], st8[:, 2:3], st8[:, 2:3], ALU.mult), sv)
                bs.inc(vector.tensor_tensor(st8[:, 5:6], st8[:, 3:4], st8[:, 4:5], ALU.subtract), sv)
            on("vector", p1_ve)
            sve = bs.snap(sv)
            ve_done.append(sve)

            def p1_se2(scalar):
                bs.wait(scalar, sve, "scalar")
                bs.inc(scalar.activation(st8[:, 6:7], st8[:, 5:6], ACT.Sqrt, bias=eps_sb[:, 0:1], scale=1.0), ss)
            on("scalar", p1_se2)
            sse2 = bs.snap(ss)

            def p1_ve2(vector, ti=ti, buf=buf):
                bs.wait(vector, sse2, "vector")
                bs.inc(vector.reciprocal(st8[:, 7:8], st8[:, 6:7]), sv)
                bs.inc(vector.tensor_scalar(lnxtm[:, ti * H:(ti + 1) * H], buf,
                                            st8[:, 2:3], st8[:, 7:8], ALU.subtract, ALU.mult), sv)
            on("vector", p1_ve2)
            ln_done.append(bs.snap(sv, ss))

        trd = bs.snap(sv)
        trs = [(hc, ti) for hc in range(NHC) for ti in range(9)]
        for qi, qn in enumerate(["sync", "scalar"]):
            part = trs[qi::2]
            strm = {"sync": s_tr, "scalar": s_trc}[qn]

            def p1_trq(eng, part=part, qn=qn, strm=strm):
                bs.wait(eng, trd, qn)
                for hc, ti in part:
                    bs.inc(eng.dma_start_transpose(
                        lnxfm[:, hc * TP + ti * 128: hc * TP + (ti + 1) * 128],
                        lnxtm[:, ti * H + hc * 128: ti * H + hc * 128 + 128]),
                        strm.sems[0], 16)
            on(qn, p1_trq)
        barrier()

    with nc.sbuf_tensor("wstg", [128, 2 * H], F32) as wstg, \
         nc.sbuf_tensor("wtl", [128, 2 * H], BF16) as wtl, \
         nc.sbuf_tensor("xpt", [128, 2 * 1026], BF16) as xpt, \
         nc.sbuf_tensor("cstg", [128, 384], F32) as cstg, \
         nc.sbuf_tensor("ctl", [128, 384], BF16) as ctl, \
         nc.sbuf_tensor("zev", [128, 2 * T], BF16) as zev:
        cast_done = [None] * 64
        mm_done = [None] * 64
        ev_done = [None] * 64
        cv_mm = [None] * 32
        cv_done = [None] * 32
        ccast_done = [None] * 32
        zsp = [None] * 32
        for m in range(64):
            is_xp = m < 32
            wb = m % 2

            def p2_gld(gpsimd, m=m, wb=wb):
                if m >= 2:
                    bs.wait(gpsimd, mm_done[m - 2], "gpsimd")
                for q in range(4):
                    g_win.dma(gpsimd, m,
                              bass.AP(wstg.ap().tensor, wb * H + q * 512,
                                      [[2 * H, 128], [128, 4], [1, 128]]),
                              bass.AP(w_in_t.tensor, (m * 16 + q * 4) * 128 * 128,
                                      [[128, 128], [128 * 128, 4], [1, 128]]))
            on("gpsimd", p2_gld)
            dW = g_win.snap(m)

            def p2_cast(vector, m=m, wb=wb):
                bs.wait(vector, dW, "vector")
                if m >= 2:
                    bs.wait(vector, mm_done[m - 2], "vector")
                bs.inc(vector.tensor_copy(wtl[:, wb * H:wb * H + H], wstg[:, wb * H:wb * H + H]), sv)
            on("vector", p2_cast)
            cast_done[m] = bs.snap(sv)

            pslice = psum[:, wb * 1536:wb * 1536 + 1026]

            def p2_mm(tensor, m=m, is_xp=is_xp, wb=wb, pslice=pslice):
                bs.wait(tensor, cast_done[m], "tensor")
                if m >= 2:
                    bs.wait(tensor, ev_done[m - 2], "tensor")
                i2 = None
                for k in range(16):
                    lhsT = wtl[:, wb * H + k * 128: wb * H + (k + 1) * 128]
                    if is_xp:
                        i2 = mmn(tensor, pslice, lhsT,
                                 bass.AP(lnxfm.ap().tensor, k * TP,
                                         [[NHC * TP, 128], [1, 1026]]),
                                 (k == 0), (k == 15), 1026)
                    else:
                        i2 = mmn(tensor, pslice, lhsT,
                                 bass.AP(lnxfm.ap().tensor, k * TP + 1,
                                         [[NHC * TP, 128], [1, 1024]]),
                                 (k == 0), (k == 15), 1024)
                bs.inc(i2, stt)
            on("tensor", p2_mm)
            mm_done[m] = bs.snap(stt)

            if is_xp:
                j = m
                xslice = xpt[:, (j % 2) * 1026:(j % 2) * 1026 + 1026]

                def p2_ev(vector, j=j, xslice=xslice, pslice=pslice):
                    bs.wait(vector, mm_done[j], "vector")
                    if j >= 2:
                        bs.wait(vector, cv_mm[j - 2], "vector")
                    bs.inc(vector.tensor_copy(xslice, pslice), sv)
                on("vector", p2_ev)
                ev_done[m] = bs.snap(sv)

                def p2_cld(gpsimd, j=j):
                    if j >= 1:
                        bs.wait(gpsimd, ccast_done[j - 1], "gpsimd")
                    g_conv.dma(gpsimd, 0, cstg[:, 0:384],
                               bass.AP(conv_diag.tensor, j * 3 * 128 * 128,
                                       [[128, 128], [128 * 128, 3], [1, 128]]))
                on("gpsimd", p2_cld)
                dC = g_conv.snap(0)

                def p2_ccast(vector, j=j):
                    bs.wait(vector, dC, "vector")
                    if j >= 1:
                        bs.wait(vector, cv_mm[j - 1], "vector")
                    bs.inc(vector.tensor_copy(ctl[:, 0:384], cstg[:, 0:384]), sv)
                on("vector", p2_ccast)
                ccast_done[j] = bs.snap(sv)

                cps = psum[:, 3072:4096]

                def p2_conv(tensor, j=j, xslice=xslice, cps=cps):
                    bs.wait(tensor, ccast_done[j], "tensor")
                    bs.wait(tensor, ev_done[j], "tensor")
                    if j >= 1:
                        bs.wait(tensor, cv_done[j - 1], "tensor")
                    i = None
                    for tap in range(3):
                        i = mmn(tensor, cps, ctl[:, tap * 128:(tap + 1) * 128],
                                xslice[:, tap:tap + 1024],
                                (tap == 0), (tap == 2), 1024)
                    bs.inc(i, stt)
                on("tensor", p2_conv)
                cv_mm[j] = bs.snap(stt)

                def p2_silu(scalar, j=j, cps=cps):
                    bs.wait(scalar, cv_mm[j], "scalar")
                    bs.inc(scalar.activation(xc_sb[:, j * T:(j + 1) * T], cps,
                                             ACT.Silu, bias=cvb_sb[:, j:j + 1], scale=1.0), ss)
                on("scalar", p2_silu)
                cv_done[j] = bs.snap(ss)

            else:
                j = m - 32
                zslice = zev[:, (j % 2) * T:(j % 2) * T + T]

                def p2_zev(scalar, j=j, m=m, pslice=pslice, zslice=zslice):
                    bs.wait(scalar, mm_done[m], "scalar")
                    if j >= 2:
                        bs.wait(scalar, zsp[j - 2], "scalar")
                    bs.inc(scalar.activation(zslice, pslice[:, 0:1024],
                                             ACT.Silu, bias=0.0, scale=1.0), ss)
                on("scalar", p2_zev)
                ev_done[m] = bs.snap(ss)

                def p2_zsp(sync, j=j, m=m, zslice=zslice):
                    bs.wait(sync, ev_done[m], "sync")
                    s_zsp.dma(sync, j, row(sp_zs, j), zslice)
                on("sync", p2_zsp)
                zsp[j] = s_zsp.snap(j)
        barrier()
    lnx_sc.__exit__(None, None, None)

    Crep_sc = nc.sbuf_tensor("Crep", [128, S * T], BF16)
    Brep_sc = nc.sbuf_tensor("Brep", [128, S * T], BF16)
    Crep = Crep_sc.__enter__()
    Brep = Brep_sc.__enter__()

    with nc.sbuf_tensor("wxs", [128, 2 * 160], F32) as wxs, \
         nc.sbuf_tensor("wxb", [128, 4 * 160], BF16) as wxb, \
         nc.sbuf_tensor("bcs", [32, T], F32) as bcs, \
         nc.sbuf_tensor("bcb", [32, T], BF16) as bcb, \
         nc.sbuf_tensor("bcf", [1, S * T], BF16) as bcf:
        wx_cast = [None] * NDC
        wx_mm = [None] * NDC
        for j in range(NDC):
            def p3_ld(gpsimd, j=j):
                if j >= 2:
                    bs.wait(gpsimd, wx_cast[j - 2], "gpsimd")
                g_wx.dma(gpsimd, j, wxs[:, (j % 2) * 160:(j % 2) * 160 + 160],
                         w_x[j * 128:(j + 1) * 128, :])
            on("gpsimd", p3_ld)
            dwx = g_wx.snap(j)

            def p3_cast(vector, j=j):
                bs.wait(vector, dwx, "vector")
                if j >= 4:
                    bs.wait(vector, wx_mm[j - 4], "vector")
                bs.inc(vector.tensor_copy(wxb[:, (j % 4) * 160:(j % 4) * 160 + 160],
                                          wxs[:, (j % 2) * 160:(j % 2) * 160 + 160]), sv)
            on("vector", p3_cast)
            wx_cast[j] = bs.snap(sv)

            def p3_mm(tensor, j=j):
                bs.wait(tensor, wx_cast[j], "tensor")
                lt = wxb[:, (j % 4) * 160:(j % 4) * 160 + 160]
                mmn(tensor, psum[:, 0:1024], lt[:, 0:128], xc_sb[:, j * T:(j + 1) * T],
                    (j == 0), (j == NDC - 1), 1024)
                i = mmn(tensor, psum[0:32, 1024:2048], lt[:, 128:160],
                        xc_sb[:, j * T:(j + 1) * T],
                        (j == 0), (j == NDC - 1), 1024)
                bs.inc(i, stt)
            on("tensor", p3_mm)
            wx_mm[j] = bs.snap(stt)
        mmx = bs.snap(stt)

        def p3_ev(vector):
            bs.wait(vector, mmx, "vector")
            bs.inc(vector.tensor_copy(dtin_sb[:], psum[:, 0:1024]), sv)
            bs.inc(vector.tensor_copy(bcs[:, :], psum[0:32, 1024:2048]), sv)
        on("vector", p3_ev)
        sev = bs.snap(sv)

        def p3_cast2(vector):
            bs.wait(vector, sev, "vector")
            bs.inc(vector.tensor_copy(bcb[:, :], bcs[:, :]), sv)
        on("vector", p3_cast2)
        sbc = bs.snap(sv)

        last_rep = None
        rep_ev = [None] * 32
        for half, dst in ((0, Brep), (1, Crep)):
            def p3_fl(sync, half=half, last_rep=last_rep):
                bs.wait(sync, sbc, "sync")
                if last_rep is not None:
                    bs.wait(sync, last_rep, "sync")
                s_bcf.dma(sync, 0, bcf[0:1, :], bcb[half * 16:half * 16 + 16, 0:T])
            on("sync", p3_fl)
            dfl = s_bcf.snap(0)

            for q in range(16):
                qb = q % 2
                gq = half * 16 + q

                def p3_rep(tensor, q=q, qb=qb, gq=gq, dfl=dfl):
                    bs.wait(tensor, dfl, "tensor")
                    if gq >= 2:
                        bs.wait(tensor, rep_ev[gq - 2], "tensor")
                    bs.inc(mmn(tensor, psum[:, 2048 + qb * 1024: 2048 + qb * 1024 + 1024],
                               ones_sb[0:1, :], bcf[0:1, q * 1024:(q + 1) * 1024],
                               True, True, 1024), stt)
                on("tensor", p3_rep)
                mr = bs.snap(stt)

                def p3_rev(vector, q=q, qb=qb, dst=dst, mr=mr):
                    bs.wait(vector, mr, "vector")
                    bs.inc(vector.tensor_copy(dst[:, q * 1024:(q + 1) * 1024],
                                              psum[:, 2048 + qb * 1024: 2048 + qb * 1024 + 1024]), sv)
                on("vector", p3_rev)
                rep_ev[gq] = bs.snap(sv)
            last_rep = bs.snap(stt)
        barrier()

    with nc.sbuf_tensor("wdts", [128, 2 * 128], F32) as wdts, \
         nc.sbuf_tensor("wdtb", [128, 2 * 128], BF16) as wdtb, \
         nc.sbuf_tensor("dtj", [128, 2 * T], F32) as dtj, \
         nc.sbuf_tensor("cj", [128, 2 * T], F32) as cj, \
         nc.sbuf_tensor("vj", [128, T], BF16) as vj, \
         nc.sbuf_tensor("xcDt", [128, T], BF16) as xcDt, \
         nc.sbuf_tensor("abuf", [128, 12 * T], BF16) as abuf, \
         nc.sbuf_tensor("bbuf", [128, T], BF16) as bbuf, \
         nc.sbuf_tensor("hbuf", [128, T], BF16) as hbuf, \
         nc.sbuf_tensor("hCbuf", [128, 4 * T], BF16) as hCbuf, \
         nc.sbuf_tensor("y0ev", [128, 2 * T], BF16) as y0ev:
        vj_done = [None] * NDC
        mmreds = [None] * NDC
        wc_done = [None] * NDC
        y0sp = [None] * NDC
        csp = [None] * NDC
        dtev = [None] * NDC
        scan_done = {}
        a_done = {}
        L_done = {}
        hC_done = {}
        red_done = {}
        y0ev_done = [None] * NDC

        def prev_key(j, s, k=4):
            if s - k >= 0:
                return (j, s - k)
            if j - 1 >= 0:
                return (j - 1, S - k + s)
            return None

        def emit_dthead(j):
            jb = j % 2
            def p5_ld(gpsimd, j=j, jb=jb):
                if j >= 2:
                    bs.wait(gpsimd, wc_done[j - 2], "gpsimd")
                g_wdt.dma(gpsimd, j, wdts[:, jb * 128:jb * 128 + 128],
                          bass.AP(w_dt_t.tensor, j * 128 * 128, [[128, 128], [1, 128]]))
            on("gpsimd", p5_ld)
            dxc = g_wdt.snap(j)

            def p5_wcast(vector, j=j, jb=jb):
                bs.wait(vector, dxc, "vector")
                bs.inc(vector.tensor_copy(wdtb[:, jb * 128:jb * 128 + 128],
                                          wdts[:, jb * 128:jb * 128 + 128]), sv)
            on("vector", p5_wcast)
            wc = bs.snap(sv)
            wc_done[j] = wc

            dtps = psum[:, 2048 + jb * 1024: 2048 + jb * 1024 + 1024]

            def p5_dtmm(tensor, j=j, dtps=dtps):
                bs.wait(tensor, wc, "tensor")
                if j >= 2:
                    bs.wait(tensor, dtev[j - 2], "tensor")
                bs.inc(mmn(tensor, dtps, wdtb[:, (j % 2) * 128:(j % 2) * 128 + 128],
                           dtin_sb[:, :], True, True, 1024), stt)
            on("tensor", p5_dtmm)
            mdt = bs.snap(stt)

            return dtps, mdt

        def emit_dttail(j, dtps, mdt):
            jb = j % 2
            def p5_sp1(scalar, j=j, jb=jb, dtps=dtps):
                bs.wait(scalar, mdt, "scalar")
                if j >= 2:
                    bs.wait(scalar, vj_done[j - 2], "scalar")
                    bs.wait(scalar, a_done[(j - 2, S - 1)], "scalar")
                bs.inc(scalar.activation(dtj[:, jb * T:jb * T + T], dtps, ACT.Exp,
                                         bias=bdt_sb[:, j:j + 1], scale=1.0), ss)
            on("scalar", p5_sp1)
            sexp = bs.snap(ss)

            def p5_sp2(vector, j=j, jb=jb, sexp=sexp):
                bs.wait(vector, sexp, "vector")
                bs.inc(vector.tensor_scalar(dtj[:, jb * T:jb * T + T],
                                            dtj[:, jb * T:jb * T + T], 1.0, None, ALU.add), sv)
            on("vector", p5_sp2)
            vp1 = bs.snap(sv)

            def p5_sp3(scalar, j=j, jb=jb, vp1=vp1):
                bs.wait(scalar, vp1, "scalar")
                bs.inc(scalar.activation(dtj[:, jb * T:jb * T + T], dtj[:, jb * T:jb * T + T],
                                         ACT.Ln, bias=0.0, scale=1.0), ss)
            on("scalar", p5_sp3)
            ssp = bs.snap(ss)
            dtev[j] = ssp

            def p5_cscan(vector, j=j, jb=jb):
                bs.wait(vector, ssp, "vector")
                if j >= 2:
                    bs.wait(vector, csp[j - 2], "vector")
                zrow = bass.AP(zcol_sb.ap().tensor, 0, [[1, 128], [0, T]])
                bs.inc(vector.tensor_tensor_scan(cj[:, jb * T:jb * T + T],
                                                 dtj[:, jb * T:jb * T + T], zrow,
                                                 zcol_sb[:, 0:1], ALU.add, ALU.add), sv)
                bs.inc(vector.tensor_copy(ctot_sb[:, j:j + 1],
                                          cj[:, jb * T + T - 1: jb * T + T]), sv)
                if j >= 1 and mmreds[j - 1] is not None:
                    bs.wait(vector, mmreds[j - 1], "vector")
                bs.inc(vector.tensor_tensor(vj[:, :], dtj[:, jb * T:jb * T + T],
                                            xc_sb[:, j * T:(j + 1) * T], ALU.mult), sv)
                bs.inc(vector.tensor_scalar(xcDt[:, :], xc_sb[:, j * T:(j + 1) * T],
                                            D_sb[:, j:j + 1], None, ALU.mult), sv)
            on("vector", p5_cscan)
            vj_done[j] = bs.snap(sv)

            def p5_csp(sync, j=j, jb=jb):
                bs.wait(sync, vj_done[j], "sync")
                s_csp.dma(sync, j, row(sp_c, j), cj[:, jb * T:jb * T + T])
            on("sync", p5_csp)
            csp[j] = s_csp.snap(j)


        def emit_sloop(j):
            jb = j % 2
            ssp = dtev[j]
            for s in range(S):
                sb = s % 4

                def p5_a(scalar, j=j, jb=jb, s=s):
                    bs.wait(scalar, ssp, "scalar")
                    pka = prev_key(j, s, 12)
                    if pka is not None and pka in scan_done:
                        bs.wait(scalar, scan_done[pka], "scalar")
                    gi = (j * S + s) % 12
                    bs.inc(scalar.activation(abuf[:, gi * T:gi * T + T],
                                             dtj[:, jb * T:jb * T + T], ACT.Exp,
                                             bias=0.0, scale=A_sb[:, j * S + s:j * S + s + 1]), ss)
                on("scalar", p5_a)
                a_done[(j, s)] = bs.snap(ss)

                def p5_scan(vector, j=j, jb=jb, s=s, sb=sb):
                    bs.wait(vector, a_done[(j, s)], "vector")
                    pk = prev_key(j, s)
                    if pk is not None and pk in red_done:
                        bs.wait(vector, red_done[pk], "vector")
                    bs.inc(vector.tensor_tensor(bbuf[:, :], vj[:, :],
                                                Brep[:, s * T:(s + 1) * T], ALU.mult), sv)
                    gi = (j * S + s) % 12
                    bs.inc(vector.tensor_tensor_scan(hbuf[:, 0:T],
                                                     abuf[:, gi * T:gi * T + T],
                                                     bbuf[:, :],
                                                     zcol_sb[:, 0:1], ALU.mult, ALU.add), sv)
                    bs.inc(vector.tensor_copy(L_sb[:, j * S + s:j * S + s + 1],
                                              hbuf[:, T - 1: T]), sv)
                    bs.inc(vector.tensor_tensor(hCbuf[:, sb * T:sb * T + T],
                                                hbuf[:, 0:T],
                                                Crep[:, s * T:(s + 1) * T], ALU.mult), sv)
                on("vector", p5_scan)
                scan_done[(j, s)] = bs.snap(sv)
                L_done[(j, s)] = scan_done[(j, s)]

                def p5_red(tensor, j=j, jb=jb, s=s, sb=sb):
                    bs.wait(tensor, scan_done[(j, s)], "tensor")
                    if s == 0 and j >= 2:
                        bs.wait(tensor, y0ev_done[j - 2], "tensor")
                    bs.inc(mmn(tensor, psum[:, jb * 1024: jb * 1024 + 1024], eye_sb[:, :],
                               hCbuf[:, sb * T:sb * T + T],
                               (s == 0), False, 1024), stt)
                on("tensor", p5_red)
                red_done[(j, s)] = bs.snap(stt)

            def p5_redD(tensor, j=j, jb=jb):
                bs.wait(tensor, vj_done[j], "tensor")
                bs.inc(mmn(tensor, psum[:, jb * 1024: jb * 1024 + 1024], eye_sb[:, :],
                           xcDt[:, :], False, True, 1024), stt)
            on("tensor", p5_redD)
            mmred = bs.snap(stt)
            mmreds[j] = mmred

            def p5_y0ev(vector, j=j, jb=jb, mmred=mmred):
                bs.wait(vector, mmred, "vector")
                if j >= 2:
                    bs.wait(vector, y0sp[j - 2], "vector")
                bs.inc(vector.tensor_copy(y0ev[:, jb * T:jb * T + T],
                                          psum[:, jb * 1024: jb * 1024 + 1024]), sv)
            on("vector", p5_y0ev)
            y0ev_done[j] = bs.snap(sv)

            def p5_y0sp(sync, j=j, jb=jb):
                bs.wait(sync, y0ev_done[j], "sync")
                s_y0sp.dma(sync, j, row(sp_y0, j), y0ev[:, jb * T:jb * T + T])
            on("sync", p5_y0sp)
            y0sp[j] = s_y0sp.snap(j)

        dtps0, mdt0 = emit_dthead(0)
        emit_dttail(0, dtps0, mdt0)
        pend = None
        for j in range(NDC):
            if j + 1 < NDC:
                pend = emit_dthead(j + 1)
            emit_sloop(j)
            if j + 1 < NDC:
                emit_dttail(j + 1, *pend)
        barrier()
    Brep_sc.__exit__(None, None, None)

    with nc.sbuf_tensor("argP", [128, NDC * S], F32) as argP, \
         nc.sbuf_tensor("Ptot", [128, NDC * S], F32) as Ptot, \
         nc.sbuf_tensor("Lk", [128, NCORES * 512], F32) as Lk, \
         nc.sbuf_tensor("Pk", [128, NCORES * 512], F32) as Pk, \
         nc.sbuf_tensor("run6", [128, 512], F32) as run6, \
         nc.sbuf_tensor("tm6", [128, 512], F32) as tm6:

        def p6_arg(vector):
            ctot_b = bass.AP(ctot_sb.ap().tensor, ctot_sb.ap().offset,
                             [[NDC, 128], [1, NDC], [0, S]])
            bs.inc(vector.tensor_tensor(argP[:, :], ctot_b, A_sb[:, :], ALU.mult), sv)
        on("vector", p6_arg)
        va = bs.snap(sv)

        def p6_exp(scalar):
            bs.wait(scalar, va, "scalar")
            bs.inc(scalar.activation(Ptot[:, :], argP[:, :], ACT.Exp, bias=0.0, scale=1.0), ss)
        on("scalar", p6_exp)
        se6 = bs.snap(ss)

        def p6_cc(gpsimd):
            bs.wait(gpsimd, se6, "gpsimd")
            g_cc.dma(gpsimd, 0, cc_in.ap()[0:128, :], L_sb[:, :])
            g_cc.dma(gpsimd, 0, cc_in.ap()[128:256, :], Ptot[:, :])
            bs.wait(gpsimd, g_cc.snap(0), "gpsimd")
            bs.inc(gpsimd.collective_compute(
                "AllGather", ALU.bypass, replica_groups=[list(range(NCORES))],
                ins=[cc_in.ap().opt()], outs=[cc_out.ap().opt()]), scc, 1)
            gpsimd.wait_ge(scc, bs.cnt["scc"])
            for k in range(NCORES):
                g_cc.dma(gpsimd, 0, Lk[:, k * 512:(k + 1) * 512],
                         cc_out.ap()[k * 256:k * 256 + 128, :])
                g_cc.dma(gpsimd, 0, Pk[:, k * 512:(k + 1) * 512],
                         cc_out.ap()[k * 256 + 128:(k + 1) * 256, :])
        on("gpsimd", p6_cc)
        d6 = {**g_cc.snap(0), scc: bs.cnt["scc"]}

        def p6_pref(vector):
            bs.wait(vector, d6, "vector")
            vector.memset(hinit_sb[:, :], 0.0)
            bs.inc(vector.tensor_copy(run6[:, :], Lk[:, 0:512]), sv)
            for k in range(1, NCORES):
                bs.inc(vector.tensor_scalar(tm6[:, :], run6[:, :],
                                            mask_sb[:, k:k + 1], None, ALU.mult), sv)
                bs.inc(vector.tensor_tensor(hinit_sb[:, :], hinit_sb[:, :], tm6[:, :], ALU.add), sv)
                if k < NCORES - 1:
                    bs.inc(vector.tensor_tensor(tm6[:, :], Pk[:, k * 512:(k + 1) * 512],
                                                run6[:, :], ALU.mult), sv)
                    bs.inc(vector.tensor_tensor(run6[:, :], tm6[:, :],
                                                Lk[:, k * 512:(k + 1) * 512], ALU.add), sv)
        on("vector", p6_pref)
        barrier()

    with nc.sbuf_tensor("cj7", [128, 2 * T], F32) as cj7, \
         nc.sbuf_tensor("zs7", [128, 2 * T], BF16) as zs7, \
         nc.sbuf_tensor("y07", [128, 2 * T], BF16) as y07, \
         nc.sbuf_tensor("Pb7", [128, S * T], BF16) as Pb7, \
         nc.sbuf_tensor("co7", [128, 4 * T], BF16) as co7, \
         nc.sbuf_tensor("yC7", [128, 4 * T], BF16) as yC7, \
         nc.sbuf_tensor("yt7", [128, T], F32) as yt7, \
         nc.sbuf_tensor("ye7", [128, 2 * T], BF16) as ye7:
        ld7 = [None] * NDC
        P7 = {}
        yC7d = {}
        red7d = {}
        yev7 = [None] * NDC
        ysp7 = [None] * NDC

        def pk7(j, s, k=4):
            if s - k >= 0:
                return (j, s - k)
            if j - 1 >= 0:
                return (j - 1, S - k + s)
            return None

        for j in range(NDC):
            jb = j % 2

            def p7_ld(gpsimd, j=j, jb=jb):
                if j >= 2:
                    bs.wait(gpsimd, yev7[j - 2], "gpsimd")
                g_p7.dma(gpsimd, j, cj7[:, jb * T:jb * T + T], row(sp_c, j))
                g_p7.dma(gpsimd, j, zs7[:, jb * T:jb * T + T], row(sp_zs, j))
                g_p7.dma(gpsimd, j, y07[:, jb * T:jb * T + T], row(sp_y0, j))
            on("gpsimd", p7_ld)
            ld7[j] = g_p7.snap(j)

            for s in range(S):
                sb = s % 4

                def p7_P(scalar, j=j, jb=jb, s=s):
                    bs.wait(scalar, ld7[j], "scalar")
                    if j >= 1 and (j - 1, s) in yC7d:
                        bs.wait(scalar, yC7d[(j - 1, s)], "scalar")
                    bs.inc(scalar.activation(Pb7[:, s * T:s * T + T],
                                             cj7[:, jb * T:jb * T + T], ACT.Exp,
                                             bias=0.0, scale=A_sb[:, j * S + s:j * S + s + 1]), ss)
                on("scalar", p7_P)
                P7[(j, s)] = bs.snap(ss)

                def p7_y1(vector, j=j, s=s, sb=sb):
                    bs.wait(vector, P7[(j, s)], "vector")
                    pk = pk7(j, s)
                    if pk is not None and pk in red7d:
                        bs.wait(vector, red7d[pk], "vector")
                    bs.inc(vector.tensor_scalar(co7[:, sb * T:sb * T + T],
                                                Pb7[:, s * T:s * T + T],
                                                hinit_sb[:, j * S + s:j * S + s + 1],
                                                None, ALU.mult), sv)
                    bs.inc(vector.tensor_tensor(yC7[:, sb * T:sb * T + T],
                                                co7[:, sb * T:sb * T + T],
                                                Crep[:, s * T:(s + 1) * T], ALU.mult), sv)
                on("vector", p7_y1)
                yC7d[(j, s)] = bs.snap(sv)

                def p7_red(tensor, j=j, jb=jb, s=s, sb=sb):
                    bs.wait(tensor, yC7d[(j, s)], "tensor")
                    if s == 0 and j >= 2:
                        bs.wait(tensor, yev7[j - 2], "tensor")
                    bs.inc(mmn(tensor, psum[:, jb * 1024: jb * 1024 + 1024], eye_sb[:, :],
                               yC7[:, sb * T:sb * T + T],
                               (s == 0), (s == S - 1), 1024), stt)
                on("tensor", p7_red)
                red7d[(j, s)] = bs.snap(stt)
            red7 = bs.snap(stt)

            def p7_y(vector, j=j, jb=jb, red7=red7):
                bs.wait(vector, red7, "vector")
                bs.wait(vector, ld7[j], "vector")
                if j >= 2:
                    bs.wait(vector, ysp7[j - 2], "vector")
                bs.inc(vector.tensor_tensor(yt7[:, :], y07[:, jb * T:jb * T + T],
                                            psum[:, jb * 1024: jb * 1024 + 1024], ALU.add), sv)
                bs.inc(vector.tensor_tensor(ye7[:, jb * T:jb * T + T], yt7[:, :],
                                            zs7[:, jb * T:jb * T + T], ALU.mult), sv)
            on("vector", p7_y)
            yev7[j] = bs.snap(sv)

            def p7_ysp(sync, j=j, jb=jb):
                bs.wait(sync, yev7[j], "sync")
                s_ysp.dma(sync, j, row(sp_y, j), ye7[:, jb * T:jb * T + T])
            on("sync", p7_ysp)
            ysp7[j] = s_ysp.snap(j)
        barrier()
    Crep_sc.__exit__(None, None, None)
    xc_sc.__exit__(None, None, None)

    lnx2_sc = nc.sbuf_tensor("lnx2fm", [128, NHC * T], BF16)
    lnx2fm = lnx2_sc.__enter__()
    out1_sc = nc.sbuf_tensor("out1", [128, NTC * H], F32)
    out1 = out1_sc.__enter__()

    with nc.sbuf_tensor("ych", [128, 4 * T], BF16) as ych, \
         nc.sbuf_tensor("wos", [128, 4 * 512], F32) as wos, \
         nc.sbuf_tensor("wob", [128, 4 * 512], BF16) as wob, \
         nc.sbuf_tensor("xres", [128, 2 * 512], F32) as xres:
        evhq = [None] * 4
        ycast = {}
        mm7b = {}
        mmts = {}
        evts = {}
        for hq in range(4):
            for j in range(NDC):
                ib = (hq * NDC + j) % 4

                def p7b_ld(gpsimd, hq=hq, j=j, ib=ib):
                    key = hq * NDC + j - 4
                    if key >= 0:
                        bs.wait(gpsimd, mm7b[key], "gpsimd")
                    g_wo.dma(gpsimd, hq * NDC + j, ych[:, ib * T:ib * T + T], row(sp_y, j))
                    g_wo.dma(gpsimd, hq * NDC + j,
                             wos[:, ib * 512:ib * 512 + 512],
                             w_out[j * 128:(j + 1) * 128, hq * 512:(hq + 1) * 512])
                on("gpsimd", p7b_ld)
                dwo = g_wo.snap(hq * NDC + j)

                def p7b_cast(vector, hq=hq, j=j, ib=ib, dwo=dwo):
                    bs.wait(vector, dwo, "vector")
                    key = hq * NDC + j - 4
                    if key >= 0:
                        bs.wait(vector, mm7b[key], "vector")
                    bs.inc(vector.tensor_copy(wob[:, ib * 512:ib * 512 + 512],
                                              wos[:, ib * 512:ib * 512 + 512]), sv)
                on("vector", p7b_cast)
                ycast[hq * NDC + j] = bs.snap(sv)

                def p7b_mm(tensor, hq=hq, j=j, ib=ib):
                    bs.wait(tensor, ycast[hq * NDC + j], "tensor")
                    if j == 0 and hq >= 1:
                        bs.wait(tensor, evhq[hq - 1], "tensor")
                    for ts in range(NTC):
                        i = tensor.matmul(psum[:, ts * 512:(ts + 1) * 512],
                                          ych[:, ib * T + ts * 128: ib * T + (ts + 1) * 128],
                                          wob[:, ib * 512:ib * 512 + 512],
                                          start=(j == 0), stop=(j == NDC - 1))
                        if j == NDC - 1:
                            bs.inc(i, stt)
                            mmts[(hq, ts)] = bs.snap(stt)
                    if j != NDC - 1:
                        bs.inc(i, stt)
                on("tensor", p7b_mm)
                mm7b[hq * NDC + j] = bs.snap(stt)
            mmw = bs.snap(stt)

            for ts in range(NTC):
                def p7b_xld(sync, hq=hq, ts=ts):
                    if (hq * NTC + ts) >= 2:
                        pts = (hq * NTC + ts) - 2
                        bs.wait(sync, evts[(pts // NTC, pts % NTC)], "sync")
                    s_xres.dma(sync, hq * NTC + ts,
                               xres[:, (ts % 2) * 512:(ts % 2) * 512 + 512],
                               x_pad[1 + ts * 128: 1 + (ts + 1) * 128,
                                     hq * 512:(hq + 1) * 512])
                on("sync", p7b_xld)
                dx = s_xres.snap(hq * NTC + ts)

                def p7b_ev(vector, hq=hq, ts=ts, mmw=mmw, dx=dx):
                    bs.wait(vector, mmts[(hq, ts)], "vector")
                    bs.wait(vector, dx, "vector")
                    bs.inc(vector.tensor_tensor(
                        out1[:, ts * H + hq * 512: ts * H + hq * 512 + 512],
                        xres[:, (ts % 2) * 512:(ts % 2) * 512 + 512],
                        psum[:, ts * 512:(ts + 1) * 512], ALU.add), sv)
                on("vector", p7b_ev)
                evts[(hq, ts)] = bs.snap(sv)
            evhq[hq] = bs.snap(sv)
        barrier()

        with nc.sbuf_tensor("ln2tm", [128, NTC * H], BF16) as ln2tm, \
             nc.sbuf_tensor("sq8", [128, H], F32) as sq8, \
             nc.sbuf_tensor("st88", [128, 8], F32) as st88:
            vprev = None
            for ts in range(NTC):
                tile = out1[:, ts * H:(ts + 1) * H]

                def p8_se1(scalar, tile=tile, vprev=vprev):
                    if vprev is not None:
                        bs.wait(scalar, vprev, "scalar")
                    bs.inc(scalar.activation(sq8[:], tile, ACT.Square, bias=0.0, scale=1.0), ss)
                on("scalar", p8_se1)
                s81 = bs.snap(ss)

                def p8_ve(vector, tile=tile, s81=s81):
                    bs.inc(vector.reduce_sum(st88[:, 0:1], tile, mybir.AxisListType.X), sv)
                    bs.wait(vector, s81, "vector")
                    bs.inc(vector.reduce_sum(st88[:, 1:2], sq8[:], mybir.AxisListType.X), sv)
                    bs.inc(vector.tensor_scalar(st88[:, 2:3], st88[:, 0:1], 1.0 / H, None, ALU.mult), sv)
                    bs.inc(vector.tensor_scalar(st88[:, 3:4], st88[:, 1:2], 1.0 / H, None, ALU.mult), sv)
                    bs.inc(vector.tensor_tensor(st88[:, 4:5], st88[:, 2:3], st88[:, 2:3], ALU.mult), sv)
                    bs.inc(vector.tensor_tensor(st88[:, 5:6], st88[:, 3:4], st88[:, 4:5], ALU.subtract), sv)
                on("vector", p8_ve)
                s82 = bs.snap(sv)
                vprev = s82

                def p8_se2(scalar, s82=s82):
                    bs.wait(scalar, s82, "scalar")
                    bs.inc(scalar.activation(st88[:, 6:7], st88[:, 5:6], ACT.Sqrt, bias=eps_sb[:, 0:1], scale=1.0), ss)
                on("scalar", p8_se2)
                s83 = bs.snap(ss)

                def p8_ve2(vector, ts=ts, tile=tile, s83=s83):
                    bs.wait(vector, s83, "vector")
                    bs.inc(vector.reciprocal(st88[:, 7:8], st88[:, 6:7]), sv)
                    bs.inc(vector.tensor_scalar(ln2tm[:, ts * H:(ts + 1) * H], tile,
                                                st88[:, 2:3], st88[:, 7:8],
                                                ALU.subtract, ALU.mult), sv)
                on("vector", p8_ve2)
                s84 = bs.snap(sv)
                vprev = s84

                def p8_osp(sync, ts=ts, tile=tile, s84=s84):
                    bs.wait(sync, s84, "sync")
                    s_o1sp.dma(sync, 0, bass.AP(sp_out1.tensor, ts * 128 * H,
                                                [[H, 128], [1, H]]), tile)
                on("sync", p8_osp)

            tr8 = bs.snap(sv)
            trs8 = [(hc, ts) for hc in range(NHC) for ts in range(NTC)]
            for qi, qn in enumerate(["sync", "scalar"]):
                part = trs8[qi::2]
                strm = {"sync": s_tr, "scalar": s_trc}[qn]

                def p8_trq(eng, part=part, qn=qn, strm=strm):
                    bs.wait(eng, tr8, qn)
                    for hc, ts in part:
                        bs.inc(eng.dma_start_transpose(
                            lnx2fm[:, hc * T + ts * 128: hc * T + (ts + 1) * 128],
                            ln2tm[:, ts * H + hc * 128: ts * H + hc * 128 + 128]),
                            strm.sems[0], 16)
                on(qn, p8_trq)
            barrier()
    out1_sc.__exit__(None, None, None)

    h1_sc = nc.sbuf_tensor("h1", [128, 32 * T], BF16)
    h1 = h1_sc.__enter__()
    h2_sc = nc.sbuf_tensor("h2", [128, 32 * T], BF16)
    h2 = h2_sc.__enter__()
    with nc.sbuf_tensor("w9s", [128, 2 * 1024], F32) as w9s, \
         nc.sbuf_tensor("w9b", [128, 2 * H], BF16) as w9b:
        c9 = [None] * 32
        c9r = [None] * 64
        m9s = [None] * 32
        e9 = [None] * 32
        for m in range(32):
            mb = m % 2

            for r in range(2):
                rb = r % 2

                def p9_ld(gpsimd, m=m, r=r, rb=rb):
                    if m * 2 + r >= 2:
                        bs.wait(gpsimd, c9r[m * 2 + r - 2], "gpsimd")
                    for q in range(2):
                        g_w1.dma(gpsimd, m * 2 + r,
                                 bass.AP(w9s.ap().tensor, rb * 1024 + q * 512,
                                         [[2 * 1024, 128], [128, 4], [1, 128]]),
                                 bass.AP(w1_t.tensor, (m * 16 + r * 8 + q * 4) * 128 * 128,
                                         [[128, 128], [128 * 128, 4], [1, 128]]))
                on("gpsimd", p9_ld)
                d9 = g_w1.snap(m * 2 + r)

                def p9_cast(vector, m=m, mb=mb, r=r, rb=rb, d9=d9):
                    bs.wait(vector, d9, "vector")
                    if m >= 2 and r == 0:
                        bs.wait(vector, m9s[m - 2], "vector")
                    bs.inc(vector.tensor_copy(w9b[:, mb * H + r * 1024: mb * H + r * 1024 + 1024],
                                              w9s[:, rb * 1024: rb * 1024 + 1024]), sv)
                on("vector", p9_cast)
                c9r[m * 2 + r] = bs.snap(sv)
            c9[m] = bs.snap(sv)

            def p9_mm(tensor, m=m, mb=mb):
                bs.wait(tensor, c9[m], "tensor")
                if m >= 2:
                    bs.wait(tensor, e9[m - 2], "tensor")
                i = None
                for k in range(16):
                    i = mmn(tensor, psum[:, mb * 1024: mb * 1024 + 1024],
                            w9b[:, mb * H + k * 128: mb * H + (k + 1) * 128],
                            lnx2fm[:, k * T:(k + 1) * T],
                            (k == 0), (k == 15), 1024)
                bs.inc(i, stt)
            on("tensor", p9_mm)
            m9 = bs.snap(stt)
            m9s[m] = m9

            def p9_ev(scalar, m=m, mb=mb, m9=m9):
                bs.wait(scalar, m9, "scalar")
                bs.inc(scalar.activation(h1[:, m * T:(m + 1) * T],
                                         psum[:, mb * 1024: mb * 1024 + 1024],
                                         ACT.Relu, bias=b1_sb[:, m:m + 1], scale=1.0), ss)
            on("scalar", p9_ev)
            e9[m] = bs.snap(ss)
        barrier()

    with nc.sbuf_tensor("was9", [128, 2 * 1024], F32) as was9, \
         nc.sbuf_tensor("wab9", [128, 2 * FF], BF16) as wab9:
        ca = [None] * 32
        car = [None] * 128
        mas = [None] * 32
        ea = [None] * 32
        for m in range(32):
            mb = m % 2

            for r in range(4):
                rb = r % 2

                strm3, qn3 = (g_w2, "gpsimd") if r < 2 else (s_w2b, "sync")

                def pa_ld(eng, m=m, r=r, rb=rb, strm3=strm3, qn3=qn3):
                    if m * 4 + r >= 2:
                        bs.wait(eng, car[m * 4 + r - 2], qn3)
                    for q in range(2):
                        strm3.dma(eng, m * 2 + (r % 2),
                                 bass.AP(was9.ap().tensor, rb * 1024 + q * 512,
                                         [[2 * 1024, 128], [128, 4], [1, 128]]),
                                 bass.AP(w2_t.tensor, (m * 32 + r * 8 + q * 4) * 128 * 128,
                                         [[128, 128], [128 * 128, 4], [1, 128]]))
                on(qn3, pa_ld)
                da = strm3.snap(m * 2 + (r % 2))

                def pa_cast(vector, m=m, mb=mb, r=r, rb=rb, da=da):
                    bs.wait(vector, da, "vector")
                    if m >= 2 and r == 0:
                        bs.wait(vector, mas[m - 2], "vector")
                    bs.inc(vector.tensor_copy(
                        wab9[:, mb * FF + r * 1024: mb * FF + r * 1024 + 1024],
                        was9[:, rb * 1024: rb * 1024 + 1024]), sv)
                on("vector", pa_cast)
                car[m * 4 + r] = bs.snap(sv)
            ca[m] = bs.snap(sv)

            def pa_mm(tensor, m=m, mb=mb):
                bs.wait(tensor, ca[m], "tensor")
                if m >= 2:
                    bs.wait(tensor, ea[m - 2], "tensor")
                i = None
                for k in range(32):
                    i = mmn(tensor, psum[:, mb * 1024: mb * 1024 + 1024],
                            wab9[:, mb * FF + k * 128: mb * FF + (k + 1) * 128],
                            h1[:, k * T:(k + 1) * T],
                            (k == 0), (k == 31), 1024)
                bs.inc(i, stt)
            on("tensor", pa_mm)
            ma = bs.snap(stt)
            mas[m] = ma

            def pa_ev(scalar, m=m, mb=mb, ma=ma):
                bs.wait(scalar, ma, "scalar")
                bs.inc(scalar.activation(h2[:, m * T:(m + 1) * T],
                                         psum[:, mb * 1024: mb * 1024 + 1024],
                                         ACT.Relu, bias=b2_sb[:, m:m + 1], scale=1.0), ss)
            on("scalar", pa_ev)
            ea[m] = bs.snap(ss)
        barrier()

    with nc.sbuf_tensor("b3r_sb", [128, H], F32) as b3r_sb, \
         nc.sbuf_tensor("w3s", [128, 4 * 512], F32) as w3s, \
         nc.sbuf_tensor("w3b", [128, 4 * 512], BF16) as w3b, \
         nc.sbuf_tensor("o1r", [128, 2 * 512], F32) as o1r, \
         nc.sbuf_tensor("fin", [128, 2 * 512], F32) as fin:

        def pb_b3(sync):
            s_const.dma(sync, 0, b3r_sb[:], b3_rep[:, :])
        on("sync", pb_b3)
        db3 = s_const.snap(0)

        evqs = [None] * 4
        stts = {}
        m3ts = {}
        fev_d = {}
        cb = {}
        mm3d = {}
        for hq in range(4):
            for k in range(32):
                ib = (hq * 32 + k) % 4

                def pb_ld(gpsimd, hq=hq, k=k, ib=ib):
                    key = hq * 32 + k - 4
                    if key >= 0:
                        bs.wait(gpsimd, cb[key], "gpsimd")
                    if key >= 0:
                        bs.wait(gpsimd, mm3d[key], "gpsimd")
                    g_w3.dma(gpsimd, hq * 32 + k,
                             w3s[:, ib * 512:ib * 512 + 512],
                             w3t[k * 128:(k + 1) * 128, hq * 512:(hq + 1) * 512])
                on("gpsimd", pb_ld)
                dw3 = g_w3.snap(hq * 32 + k)

                def pb_cast(vector, hq=hq, k=k, ib=ib, dw3=dw3):
                    bs.wait(vector, dw3, "vector")
                    key = hq * 32 + k - 4
                    if key >= 0:
                        bs.wait(vector, mm3d[key], "vector")
                    bs.inc(vector.tensor_copy(w3b[:, ib * 512:ib * 512 + 512],
                                              w3s[:, ib * 512:ib * 512 + 512]), sv)
                on("vector", pb_cast)
                cb[hq * 32 + k] = bs.snap(sv)

                def pb_mm(tensor, hq=hq, k=k, ib=ib):
                    bs.wait(tensor, cb[hq * 32 + k], "tensor")
                    if k == 0 and hq >= 1:
                        bs.wait(tensor, evqs[hq - 1], "tensor")
                    for ts in range(NTC):
                        i = tensor.matmul(psum[:, ts * 512:(ts + 1) * 512],
                                          h2[:, k * T + ts * 128: k * T + (ts + 1) * 128],
                                          w3b[:, ib * 512:ib * 512 + 512],
                                          start=(k == 0), stop=(k == 31))
                        if k == 31:
                            bs.inc(i, stt)
                            m3ts[(hq, ts)] = bs.snap(stt)
                    if k != 31:
                        bs.inc(i, stt)
                on("tensor", pb_mm)
                mm3d[hq * 32 + k] = bs.snap(stt)
            m3 = bs.snap(stt)

            for ts in range(NTC):
                def pb_o1(sync, hq=hq, ts=ts):
                    if (hq * NTC + ts) >= 2:
                        bs.wait(sync, fev_d.get(hq * NTC + ts - 2, {}), "sync")
                    s_o1r.dma(sync, hq * NTC + ts,
                              o1r[:, (ts % 2) * 512:(ts % 2) * 512 + 512],
                              bass.AP(sp_out1.tensor, ts * 128 * H + hq * 512,
                                      [[H, 128], [1, 512]]))
                on("sync", pb_o1)
                do1 = s_o1r.snap(hq * NTC + ts)

                def pb_ev(vector, hq=hq, ts=ts, m3=m3, do1=do1):
                    bs.wait(vector, m3ts[(hq, ts)], "vector")
                    bs.wait(vector, do1, "vector")
                    bs.wait(vector, db3, "vector")
                    if (hq * NTC + ts) >= 2:
                        bs.wait(vector, s_out.snap(hq * NTC + ts - 2), "vector")
                    sl = fin[:, (ts % 2) * 512:(ts % 2) * 512 + 512]
                    bs.inc(vector.tensor_tensor(sl, psum[:, ts * 512:(ts + 1) * 512],
                                                b3r_sb[:, hq * 512:(hq + 1) * 512], ALU.add), sv)
                    bs.inc(vector.tensor_tensor(sl, sl,
                                                o1r[:, (ts % 2) * 512:(ts % 2) * 512 + 512],
                                                ALU.add), sv)
                on("vector", pb_ev)
                fev = bs.snap(sv)
                fev_d[hq * NTC + ts] = fev

                def pb_st(sync, hq=hq, ts=ts, fev=fev):
                    bs.wait(sync, fev, "sync")
                    s_out.dma(sync, hq * NTC + ts,
                              out[ts * 128:(ts + 1) * 128, hq * 512:(hq + 1) * 512],
                              fin[:, (ts % 2) * 512:(ts % 2) * 512 + 512])
                on("sync", pb_st)
                stts[(hq, ts)] = s_out.snap(hq * NTC + ts)
            evqs[hq] = bs.snap(sv)

        fdeps = s_out.snap_all()

        def fin_wait(gpsimd):
            bs.wait(gpsimd, fdeps, "gpsimd")
        on("gpsimd", fin_wait)
        flush_streams()
    h2_sc.__exit__(None, None, None)
    h1_sc.__exit__(None, None, None)
    lnx2_sc.__exit__(None, None, None)

    block.__exit__(None, None, None)
    return nc


_NC_CACHE = {}


def _prep(inputs):
    W_in = np.ascontiguousarray(np.asarray(inputs["L0_attn_W_in"], np.float32))
    w_in_tt = np.ascontiguousarray(W_in.reshape(16, 128, 64, 128).transpose(2, 0, 1, 3))
    conv_w = np.asarray(inputs["L0_attn_conv_w"], np.float32)[:, 0, :]
    cd = np.zeros((NDC, 3, 128, 128), np.float32)
    idx = np.arange(128)
    for j in range(NDC):
        for tap in range(3):
            cd[j, tap, idx, idx] = conv_w[j * 128:(j + 1) * 128, tap]

    def cols(v, n):
        return np.ascontiguousarray(np.asarray(v, np.float32).reshape(n, 128).T)

    shared = {
        "w_in_t": w_in_tt,
        "conv_diag": cd,
        "conv_b_t": cols(inputs["L0_attn_conv_b"], 32),
        "w_x": np.ascontiguousarray(np.asarray(inputs["L0_attn_W_x"], np.float32)),
        "w_dt_t": np.ascontiguousarray(
            np.asarray(inputs["L0_attn_W_dt"], np.float32).reshape(128, 32, 128).transpose(1, 0, 2)),
        "b_dt_t": cols(inputs["L0_attn_b_dt"], 32),
        "a_log": np.ascontiguousarray(np.asarray(inputs["L0_attn_A_log"], np.float32)),
        "d_t": cols(inputs["L0_attn_D"], 32),
        "w_out": np.ascontiguousarray(np.asarray(inputs["L0_attn_W_out"], np.float32)),
        "w1_t": np.ascontiguousarray(
            np.asarray(inputs["L0_ffn_W1"], np.float32).T.reshape(16, 128, 32, 128).transpose(2, 0, 1, 3)),
        "b1_t": cols(inputs["L0_ffn_b1"], 32),
        "w2_t": np.ascontiguousarray(
            np.asarray(inputs["L0_ffn_W2"], np.float32).T.reshape(32, 128, 32, 128).transpose(2, 0, 1, 3)),
        "b2_t": cols(inputs["L0_ffn_b2"], 32),
        "w3t": np.ascontiguousarray(np.asarray(inputs["L0_ffn_W3"], np.float32).T),
        "b3_rep": np.ascontiguousarray(
            np.tile(np.asarray(inputs["L0_ffn_b3"], np.float32)[None, :], (128, 1))),
        "eye_in": np.eye(128, dtype=np.float32),
        "ones_in": np.ones((1, 128), np.float32),
    }
    x = np.ascontiguousarray(np.asarray(inputs["L0_attn_x"], np.float32)[0])
    in_maps = []
    for c in range(NCORES):
        xp = np.zeros((TP, H), np.float32)
        xp[1:1 + T] = x[c * T:(c + 1) * T]
        if c > 0:
            xp[0] = x[c * T - 1]
        if c < NCORES - 1:
            xp[1 + T] = x[(c + 1) * T]
        cm = np.zeros((128, NCORES), np.float32)
        cm[:, c] = 1.0
        im = dict(shared)
        im["x_pad"] = xp
        im["cmask"] = cm
        in_maps.append(im)
    return in_maps


def kernel(**inputs):
    in_maps = _prep(inputs)
    if "nc" not in _NC_CACHE:
        _NC_CACHE["nc"] = build()
    res = run_bass_kernel_spmd(_NC_CACHE["nc"], in_maps, core_ids=list(range(NCORES)))
    o = np.concatenate([r["out"] for r in res.results], axis=0)
    return o.reshape(1, 8192, 2048).astype(np.float32)
